# version 01
# baseline (speedup 1.0000x reference)
import numpy as np
import ml_dtypes
import concourse.bass as bass
import concourse.mybir as mybir
from concourse.alu_op_type import AluOpType
from concourse.bass_utils import run_bass_kernel_spmd
from contextlib import ExitStack

F32 = mybir.dt.float32
BF16 = mybir.dt.bfloat16
I32 = mybir.dt.int32
U16 = mybir.dt.uint16
AF = mybir.ActivationFunctionType
AX = mybir.AxisListType
AL = AluOpType


SLOC = 256
NTOK = 4 * SLOC
D = 4096
NF = 32
NT = 8
XC = SLOC + 1
XCOLS = 4 * XC
PROJS = ["k", "r", "g", "v"]
NOG = 16
TOPK = 16

def build_A():
    nc = bass.Bass("TRN2", target_bir_lowering=False)

    def di(name, shape, dtype=F32):
        return nc.dram_tensor(name, shape, dtype, kind="ExternalInput").ap()

    def do(name, shape, dtype):
        return nc.dram_tensor(name, shape, dtype, kind="ExternalOutput").ap()

    xa = di("xa", [NTOK, D])
    xh = di("xh", [4, D])
    wT = {p: di(f"w{p}T", [D, D]) for p in PROJS}
    tmD = {p: di(f"tm{p}", [128, 32]) for p in PROJS}
    ApT = di("ApT", [D, 1024])
    Bp = di("Bp", [1024, D])
    WrT = di("WrT", [D, 64])
    idf = di("idf", [128, 128])
    idb = nc.dram_tensor("idb", [128, 128], BF16, kind="ExternalInput").ap()

    out_fm = {p: do(f"{p}T_o", [D, NTOK], BF16) for p in PROJS}
    ffn0_o = do("ffn0_o", [NTOK, D], BF16)
    xnT_o = do("xnT_o", [D, XCOLS], BF16)
    GT_o = do("GT_o", [64, NTOK], F32)

    xnT_d = nc.dram_tensor("xnT_d", [D, XCOLS], BF16).ap()
    GTrep = nc.dram_tensor("GTrep", [16 * 64, NTOK], F32).ap()

    with ExitStack() as ctx:
        E = ctx.enter_context
        AR1 = E(nc.sbuf_tensor("AR1", [128, 32 * 1024], BF16))
        AR2 = E(nc.sbuf_tensor("AR2", [128, 16384], BF16))
        AR3 = E(nc.sbuf_tensor("AR3", [128, 8192], BF16))
        AR4 = E(nc.sbuf_tensor("AR4", [128, 8 * 1024], BF16))
        AR5 = E(nc.sbuf_tensor("AR5", [128, 8 * 1024], BF16))
        AR6 = E(nc.sbuf_tensor("AR6", [128, 12544], BF16))
        XHT = E(nc.sbuf_tensor("XHT", [128, 128], BF16))
        SM = E(nc.sbuf_tensor("SM", [128, 512], F32))
        SIDB = E(nc.sbuf_tensor("SIDB", [128, 128], BF16))
        SWr = E(nc.sbuf_tensor("SWr", [128, NF * 64], BF16))
        SCO = E(nc.sbuf_tensor("SCO", [64, NTOK], F32))
        SGT = E(nc.sbuf_tensor("SGT", [64, NTOK], F32))
        TK = E(nc.sbuf_tensor("TK", [128, 256], F32))

        xat = [AR2[:, 0:8192].bitcast(F32), AR2[:, 8192:16384].bitcast(F32)]
        sqv = AR3[:, 0:8192].bitcast(F32)
        evb = [AR3[:, 0:4096], AR3[:, 4096:8192]]
        xnt = AR6[:, 0:4096]
        stg = [AR6[:, 4096:8192], AR6[:, 8192:12288]]
        xnc = [AR6[:, 0:1028], AR6[:, 1056:2084]]
        mixT = AR1
        wog = [AR2[:, 0:8192], AR2[:, 8192:16384]]
        xns = AR1
        Bp_s = AR1
        hGT = AR4
        gex = AR5

        sm_sum = SM[:, 0:1]
        sm_ss = SM[:, 1:2]
        sm_mean = SM[:, 2:3]
        sm_var = SM[:, 4:5]
        sm_rstd = SM[:, 5:6]
        sm_m2 = SM[:, 6:7]
        sm_ln = SM[:, 7:8]
        sm_eps = SM[:, 3:4]
        tmS = {p: SM[:, 8 + i * 32: 8 + (i + 1) * 32] for i, p in enumerate(PROJS)}
        sid = SM[:, 144:272]
        tk_m1n = SM[:, 282:283]
        tk_thr = SM[:, 283:284]
        tk_z = SM[:, 284:285]
        tk_rcp = SM[:, 285:286]
        tk_cur = SM[:, 286:287]
        tk_w = TK[:, 0:64]
        tk_m = TK[:, 64:128]
        tk_e = TK[:, 128:192]
        tk_o = TK[:, 192:256]

        pA = E(nc.psum_tensor("pA", [128, 2048], F32))
        pT0 = E(nc.psum_tensor("pT0", [128, 1024], BF16))
        pT1 = E(nc.psum_tensor("pT1", [128, 1024], BF16))
        pS = E(nc.psum_tensor("pS", [128, 1024], F32))

        s_ld = E(nc.semaphore("s_ld"))
        s_cst = E(nc.semaphore("s_cst"))
        s_xa = [E(nc.semaphore("s_xa0")), E(nc.semaphore("s_xa1"))]
        s_xc = [E(nc.semaphore("s_xc0")), E(nc.semaphore("s_xc1"))]
        s_w = [E(nc.semaphore("s_w0")), E(nc.semaphore("s_w1"))]
        s_wg = E(nc.semaphore("s_wg"))
        s_gt = E(nc.semaphore("s_gt"))
        s_so = E(nc.semaphore("s_so"))
        s_t = E(nc.semaphore("s_t"))
        s_v = E(nc.semaphore("s_v"))
        s_sc = E(nc.semaphore("s_sc"))
        s_st = [E(nc.semaphore("s_st0")), E(nc.semaphore("s_st1"))]
        s_bar = E(nc.semaphore("s_bar"))
        block = E(nc.Block())

        bar = [0]

        def barrier():
            bar[0] += 1
            tgt = 5 * bar[0]
            for dec in (block.sync, block.tensor, block.vector, block.scalar, block.gpsimd):
                @dec
                def _(e):
                    e.drain()
                    e.sem_inc(s_bar, 1)
                    e.wait_ge(s_bar, tgt)

        def v_var(i): return 34 * i + 1
        def v_norm(i): return 34 * i + 2
        def v_evac(i, fc): return 34 * i + 3 + fc
        def v_flush(i): return 34 * i + 34

        @block.sync
        def _(sync):
            sync.dma_start(out=sid, in_=idf).then_inc(s_cst, 16)
            sync.dma_start(out=SIDB[:], in_=idb).then_inc(s_cst, 16)
            for p in PROJS:
                sync.dma_start(out=tmS[p], in_=tmD[p]).then_inc(s_cst, 16)
            for i in range(NT + 1):
                if i >= 2:
                    sync.wait_ge(s_v, v_norm(i - 2))
                if i < NT:
                    sync.dma_start(out=xat[i % 2][:], in_=xa[i * 128:(i + 1) * 128, :]
                                   ).then_inc(s_xa[i % 2], 16)
                else:
                    sync.dma_start(out=xat[NT % 2][0:4, :], in_=xh).then_inc(s_xa[NT % 2], 16)
                j = i - 1
                if 0 <= j < NT:
                    sync.wait_ge(s_v, v_flush(j))
                    bb = j // 2
                    base = bb * XC + 1 + (j % 2) * 128
                    dst = xnT_d[:, base:base + 128].rearrange("(fc p) c -> p fc c", p=128)
                    sync.dma_start(out=dst,
                                   in_=stg[j % 2].rearrange("p (fc c) -> p fc c", c=128)
                                   ).then_inc(s_st[j % 2], 16)

        @block.scalar
        def _(sc):
            for i in range(NT + 1):
                rows = slice(0, 128) if i < NT else slice(0, 4)
                sc.wait_ge(s_v, v_var(i))
                sc.activation(sm_ln[rows, :], sm_var[rows, :], AF.Ln, bias=sm_eps[rows, 0:1])
                sc.drain()
                sc.activation(sm_rstd[rows, :], sm_ln[rows, :], AF.Exp, scale=-0.5).then_inc(s_sc, 1)

        @block.vector
        def _(v):
            v.memset(sm_eps[:, :], 1e-5)
            v.drain()
            for i in range(NT + 1):
                rows = slice(0, 128) if i < NT else slice(0, 4)
                buf = xat[i % 2] if i < NT else xat[NT % 2]
                v.wait_ge(s_cst, 96)
                v.wait_ge(s_xa[i % 2], 16 * (i // 2 + 1))
                v.tensor_tensor(sqv[rows, :], buf[rows, :], buf[rows, :], op=AL.mult)
                v.drain()
                v.tensor_reduce(sm_ss[rows, :], sqv[rows, :], axis=AX.X, op=AL.add)
                v.tensor_reduce(sm_sum[rows, :], buf[rows, :], axis=AX.X, op=AL.add)
                v.drain()
                v.tensor_scalar(sm_mean[rows, :], sm_sum[rows, :], 1.0 / D, None, op0=AL.mult)
                v.drain()
                v.tensor_tensor(sm_m2[rows, :], sm_mean[rows, :], sm_mean[rows, :], op=AL.mult)
                v.tensor_scalar(sm_var[rows, :], sm_ss[rows, :], 1.0 / D, None, op0=AL.mult)
                v.drain()
                v.tensor_tensor(sm_var[rows, :], sm_var[rows, :], sm_m2[rows, :], op=AL.subtract
                                ).then_inc(s_v, 1)
                if i >= 1:
                    v.wait_ge(s_t, 32 * i)
                v.wait_ge(s_sc, i + 1)
                v.tensor_scalar(xnt[rows, :], buf[rows, :], sm_mean[rows, 0:1], sm_rstd[rows, 0:1],
                                op0=AL.subtract, op1=AL.mult).then_inc(s_v, 1)
                if 2 <= i < NT:
                    v.wait_ge(s_st[i % 2], 16 * (i // 2))
                dst = stg[i % 2] if i < NT else XHT
                wid = 128 if i < NT else 4
                for fc in range(NF):
                    v.wait_ge(s_t, 32 * i + fc + 1)
                    pt = (pT0, pT1)[fc % 2]
                    v.tensor_copy(dst[:, fc * wid:(fc + 1) * wid], pt[:, 0:wid]).then_inc(s_v, 1)
                v.drain()

        @block.tensor
        def _(t):
            for i in range(NT + 1):
                rows = slice(0, 128) if i < NT else slice(0, 4)
                t.wait_ge(s_v, v_norm(i))
                for fc in range(NF):
                    g = 32 * i + fc
                    if g >= 2:
                        i2, fc2 = divmod(g - 2, 32)
                        t.wait_ge(s_v, v_evac(i2, fc2))
                    pt = (pT0, pT1)[fc % 2]
                    wid = 128 if i < NT else 4
                    t.transpose(pt[:, 0:wid],
                                xnt[rows, fc * 128:(fc + 1) * 128],
                                SIDB[0:wid, 0:wid]).then_inc(s_t, 1)

        barrier()
        LD1 = 16 * 15
        V1 = 34 * (NT + 1)
        T1 = 32 * (NT + 1)
        SC1 = NT + 1
        ST1 = 16 * NT
        W1 = 0

        def v_base(pi):
            tot = V1
            for q in range(pi):
                tot += 32 + (0 if PROJS[q] == "g" else NOG)
            return tot

        def sc_base(pi):
            tot = SC1
            for q in range(pi):
                tot += NOG if PROJS[q] == "g" else 0
            return tot

        def w_base(pi): return W1 + 16 * 2 * NOG * pi
        def t_base(pi): return T1 + NOG * pi
        def st_base(pi): return ST1 + 16 * 2 * NOG * pi
        def ld_base(pi): return LD1 + 16 * NF * pi

        for pi, p in enumerate(PROJS):
            is_g = (p == "g")

            @block.sync
            def _(sync, pi=pi, p=p, is_g=is_g):
                if pi == 0:
                    sync.wait_ge(s_st[0], 64)
                    sync.wait_ge(s_st[1], 64)
                for f in range(NF):
                    if pi == 0 and f < 2:
                        pass
                    else:
                        gidx = pi * NF + f
                        i2, f2 = divmod(gidx - 2, NF)
                        sync.wait_ge(s_v, v_base(i2) + f2 + 1)
                    sync.dma_start(
                        out=xnc[f % 2].rearrange("p (b u) -> p b u", b=4)[:, :, 1:XC],
                        in_=xnT_d.rearrange("d (b u) -> d b u", u=XC)[f * 128:(f + 1) * 128, :, 1:XC]
                    ).then_inc(s_xc[f % 2], 16)
                for og in range(NOG):
                    if is_g:
                        sync.wait_ge(s_sc, sc_base(pi) + og + 1)
                    else:
                        sync.wait_ge(s_v, v_base(pi) + 32 + og + 1)
                    for o in range(2):
                        oc = og * 2 + o
                        sync.dma_start(out=out_fm[p][oc * 128:(oc + 1) * 128, :],
                                       in_=evb[og % 2][:, o * 1024:(o + 1) * 1024]
                                       ).then_inc(s_st[og % 2], 16)

            @block.vector
            def _(v, pi=pi, p=p, is_g=is_g):
                for f in range(NF):
                    gidx = pi * NF + f
                    v.wait_ge(s_xc[f % 2], 16 * (gidx // 2 + 1))
                    if pi > 0 and f == 0:
                        v.wait_ge(s_t, t_base(pi))
                    src = xnc[f % 2].rearrange("p (b u) -> p b u", b=4)
                    dstv = mixT[:, f * 1024:(f + 1) * 1024].rearrange("p (b s) -> p b s", b=4)
                    v.tensor_copy(src[:, :, 0:1],
                                  XHT[:, f * 4:(f + 1) * 4].rearrange("p (b o) -> p b o", o=1))
                    v.drain()
                    v.tensor_tensor(dstv, src[:, :, 1:XC], src[:, :, 0:SLOC], op=AL.subtract)
                    v.drain()
                    v.scalar_tensor_tensor(dstv, dstv, tmS[p][:, f:f + 1], src[:, :, 0:SLOC],
                                           op0=AL.mult, op1=AL.add).then_inc(s_v, 1)
                if not is_g:
                    for og in range(NOG):
                        v.wait_ge(s_t, t_base(pi) + og + 1)
                        if pi * NOG + og >= 2:
                            prev = pi * NOG + og - 2
                            v.wait_ge(s_st[og % 2], 64 + 32 * (prev // 2 + 1))
                        for j in range(4):
                            ins = v.tensor_copy(evb[og % 2][:, j * 512:(j + 1) * 512],
                                                pA[:, j * 512:(j + 1) * 512])
                            if j == 3:
                                ins.then_inc(s_v, 1)
                        v.drain()

            if is_g:
                @block.scalar
                def _(sc, pi=pi):
                    for og in range(NOG):
                        sc.wait_ge(s_t, t_base(pi) + og + 1)
                        if pi * NOG + og >= 2:
                            prev = pi * NOG + og - 2
                            sc.wait_ge(s_st[og % 2], 64 + 32 * (prev // 2 + 1))
                        for j in range(4):
                            ins = sc.activation(evb[og % 2][:, j * 512:(j + 1) * 512],
                                                pA[:, j * 512:(j + 1) * 512], AF.Silu)
                            if j == 3:
                                ins.then_inc(s_sc, 1)
                        sc.drain()

            @block.gpsimd
            def _(g, pi=pi, p=p):
                for og in range(NOG):
                    G = pi * NOG + og
                    if G >= 2:
                        t_prev = T1 + (G - 1)
                        g.wait_ge(s_t, t_prev)
                    src_all = wT[p][:, og * 256:(og + 1) * 256].rearrange("(fg p) c -> p fg c", p=128)
                    dst_all = wog[og % 2].rearrange("p (f c) -> p f c", c=256)
                    for jj in range(2):
                        g.dma_start(out=dst_all[:, jj * 16:(jj + 1) * 16, :],
                                    in_=src_all[:, jj * 16:(jj + 1) * 16, :]).then_inc(s_w[og % 2], 16)

            @block.tensor
            def _(t, pi=pi, p=p, is_g=is_g):
                for og in range(NOG):
                    G = pi * NOG + og
                    t.wait_ge(s_w[og % 2], 32 * (G // 2 + 1))
                    if og == 0:
                        t.wait_ge(s_v, v_base(pi) + 32)
                    if og >= 1:
                        if is_g:
                            t.wait_ge(s_sc, sc_base(pi) + og)
                        else:
                            t.wait_ge(s_v, v_base(pi) + 32 + og)
                    wv = wog[og % 2].rearrange("p (f c) -> p f c", c=256)
                    for f in range(NF):
                        for o in range(2):
                            for h in range(2):
                                ins = t.matmul(pA[:, (o * 2 + h) * 512:(o * 2 + h + 1) * 512],
                                               wv[:, f, o * 128:(o + 1) * 128],
                                               mixT[:, f * 1024 + h * 512: f * 1024 + (h + 1) * 512],
                                               start=(f == 0), stop=(f == NF - 1))
                                if f == NF - 1 and o == 1 and h == 1:
                                    ins.then_inc(s_t, 1)

        barrier()
        LD2 = LD1 + 16 * NF * 4
        V2 = v_base(4)
        SC2 = sc_base(4)
        W2 = W1 + 16 * 2 * NOG * 4
        T2 = T1 + NOG * 4
        ST2 = ST1 + 16 * 2 * NOG * 4

        def tB(tb): return T2 + 1 + 2 * tb
        def vB(tb): return V2 + 1 + 3 * tb

        @block.sync
        def _(sync):
            for f in range(NF):
                src = xnT_d.rearrange("d (b u) -> d b u", u=XC)[f * 128:(f + 1) * 128, :, 1:XC]
                sync.dma_start(out=xns[:, f * 1024:(f + 1) * 1024].rearrange("p (b s) -> p b s", b=4),
                               in_=src).then_inc(s_ld, 16)
            sync.wait_ge(s_v, V2 + 1 + 3 * NT)
            grv = GTrep.rearrange("(k r) t -> k r t", r=16)
            for rep in range(16):
                sync.dma_start(out=grv[:, rep, :], in_=SGT[:]).then_inc(s_gt, 16)
            sync.dma_start(out=GT_o, in_=SGT[:]).then_inc(s_so, 16)

        @block.gpsimd
        def _(g):
            g.dma_start(out=SWr[:].rearrange("p (f c) -> p f c", c=64),
                        in_=WrT.rearrange("(f p) c -> p f c", p=128)).then_inc(s_wg, 16)
            g.wait_ge(s_gt, 16 * 16)
            for c in range(8):
                g.dma_start(out=gex[:, c * 1024:(c + 1) * 1024],
                            in_=GTrep[c * 128:(c + 1) * 128, :]).then_inc(s_wg, 16)

        @block.tensor
        def _(t):
            t.wait_ge(s_ld, 16 * NF)
            t.wait_ge(s_wg, 16)
            for h in range(2):
                for f in range(NF):
                    ins = t.matmul(pS[0:64, h * 512:(h + 1) * 512], SWr[:, f * 64:(f + 1) * 64],
                                   xns[:, f * 1024 + h * 512: f * 1024 + (h + 1) * 512],
                                   start=(f == 0), stop=(f == NF - 1))
                    if h == 1 and f == NF - 1:
                        ins.then_inc(s_t, 1)
            for tb in range(NT):
                if tb == 0:
                    t.wait_ge(s_v, V2 + 1)
                t.transpose(pA[:, 0:64], SCO[:, tb * 128:(tb + 1) * 128],
                            sid[0:64, 0:64]).then_inc(s_t, 1)
                t.wait_ge(s_v, vB(tb) + 2)
                t.transpose(pA[0:64, 512:640], tk_e, sid).then_inc(s_t, 1)

        @block.vector
        def _(v):
            v.wait_ge(s_t, T2 + 1)
            v.tensor_copy(SCO[:, 0:512], pS[0:64, 0:512])
            v.tensor_copy(SCO[:, 512:1024], pS[0:64, 512:1024]).then_inc(s_v, 1)
            v.drain()
            for tb in range(NT):
                v.wait_ge(s_t, tB(tb) + 1)
                v.tensor_copy(tk_w[:], pA[:, 0:64])
                v.tensor_copy(tk_o[:], pA[:, 0:64])
                v.drain()
                for it in range(TOPK):
                    dst = tk_thr if it == TOPK - 1 else tk_cur
                    v.tensor_reduce(dst[:], tk_w[:], axis=AX.X, op=AL.max)
                    v.drain()
                    if it == 0:
                        v.tensor_scalar(tk_m1n[:], tk_cur[:], -1.0, None, op0=AL.mult)
                        v.drain()
                    if it < TOPK - 1:
                        v.tensor_scalar(tk_m[:], tk_w[:], tk_cur[:, 0:1], None, op0=AL.is_ge)
                        v.drain()
                        v.tensor_scalar(tk_m[:], tk_m[:], -3.0e38, None, op0=AL.mult)
                        v.drain()
                        v.tensor_tensor(tk_w[:], tk_w[:], tk_m[:], op=AL.add)
                        v.drain()
                v.tensor_copy(tk_m[:], tk_w[:]).then_inc(s_v, 1)
                v.wait_ge(s_sc, SC2 + tb + 1)
                v.tensor_scalar(tk_m[:], tk_o[:], tk_thr[:, 0:1], None, op0=AL.is_ge)
                v.drain()
                v.tensor_tensor(tk_e[:], tk_e[:], tk_m[:], op=AL.mult)
                v.drain()
                v.tensor_reduce(tk_z[:], tk_e[:], axis=AX.X, op=AL.add)
                v.drain()
                v.reciprocal(tk_rcp[:], tk_z[:])
                v.drain()
                v.tensor_scalar(tk_e[:], tk_e[:], tk_rcp[:, 0:1], None, op0=AL.mult
                                ).then_inc(s_v, 1)
                v.drain()
                v.wait_ge(s_t, tB(tb) + 2)
                v.tensor_copy(SGT[:, tb * 128:(tb + 1) * 128], pA[0:64, 512:640]
                              ).then_inc(s_v, 1)
                v.drain()

        @block.scalar
        def _(sc):
            for tb in range(NT):
                sc.wait_ge(s_v, vB(tb) + 1)
                sc.activation(tk_e[:], tk_o[:], AF.Exp, bias=tk_m1n[:, 0:1]).then_inc(s_sc, 1)

        LD3 = LD2 + 16 * NF
        V3 = V2 + 1 + 3 * NT
        SC3 = SC2 + NT
        W3 = W2 + 16 + 16 * 8
        T3 = T2 + 1 + 2 * NT
        ST3 = ST2 + 16 * 17

        @block.gpsimd
        def _(g):
            for krg in range(4):
                if krg >= 2:
                    g.wait_ge(s_t, T3 + krg - 1)
                src_all = ApT[:, krg * 256:(krg + 1) * 256].rearrange("(fg p) c -> p fg c", p=128)
                dst_all = wog[krg % 2].rearrange("p (f c) -> p f c", c=256)
                for jj in range(2):
                    g.dma_start(out=dst_all[:, jj * 16:(jj + 1) * 16, :],
                                in_=src_all[:, jj * 16:(jj + 1) * 16, :]).then_inc(s_w[krg % 2], 16)

        @block.tensor
        def _(t):
            for krg in range(4):
                t.wait_ge(s_w[krg % 2], 32 * 32 + 32 * (krg // 2 + 1))
                if krg >= 1:
                    t.wait_ge(s_v, V3 + krg)
                wv = wog[krg % 2].rearrange("p (f c) -> p f c", c=256)
                for f in range(NF):
                    for kr in range(2):
                        for h in range(2):
                            ins = t.matmul(pA[:, (kr * 2 + h) * 512:(kr * 2 + h + 1) * 512],
                                           wv[:, f, kr * 128:(kr + 1) * 128],
                                           xns[:, f * 1024 + h * 512: f * 1024 + (h + 1) * 512],
                                           start=(f == 0), stop=(f == NF - 1))
                            if f == NF - 1 and kr == 1 and h == 1:
                                ins.then_inc(s_t, 1)

        @block.vector
        def _(v):
            v.wait_ge(s_wg, 16 + 16 * 8)
            for krg in range(4):
                v.wait_ge(s_t, T3 + krg + 1)
                for kr in range(2):
                    kc = krg * 2 + kr
                    for h in range(2):
                        ins = v.tensor_tensor(hGT[:, kc * 1024 + h * 512: kc * 1024 + (h + 1) * 512],
                                              pA[:, (kr * 2 + h) * 512:(kr * 2 + h + 1) * 512],
                                              gex[:, kc * 1024 + h * 512: kc * 1024 + (h + 1) * 512],
                                              op=AL.mult)
                        if kr == 1 and h == 1:
                            ins.then_inc(s_v, 1)
                v.drain()

        T4 = T3 + 4
        V4 = V3 + 4
        W4 = W3 + 16 * 8

        barrier()

        @block.gpsimd
        def _(g):
            for kr in range(8):
                g.dma_start(out=Bp_s[:, kr * 4096:(kr + 1) * 4096],
                            in_=Bp[kr * 128:(kr + 1) * 128, :]).then_inc(s_wg, 16)

        @block.tensor
        def _(t):
            t.wait_ge(s_wg, 16 * 17)
            for grp in range(16):
                tc, oh = divmod(grp, 2)
                if grp >= 1:
                    t.wait_ge(s_sc, SC3 + grp)
                for kr in range(8):
                    for j in range(4):
                        ins = t.matmul(pA[:, j * 512:(j + 1) * 512],
                                       hGT[:, kr * 1024 + tc * 128: kr * 1024 + (tc + 1) * 128],
                                       Bp_s[:, kr * 4096 + oh * 2048 + j * 512: kr * 4096 + oh * 2048 + (j + 1) * 512],
                                       start=(kr == 0), stop=(kr == 7))
                        if kr == 7 and j == 3:
                            ins.then_inc(s_t, 1)

        @block.scalar
        def _(sc):
            for grp in range(16):
                sc.wait_ge(s_t, T4 + grp + 1)
                if grp >= 2:
                    sc.wait_ge(s_st[grp % 2], 1088 + 16 * (grp // 2))
                else:
                    sc.wait_ge(s_st[grp % 2], 1088)
                sc.activation(evb[grp % 2][:, 0:2048], pA[:, 0:2048], AF.Copy, scale=2.0
                              ).then_inc(s_sc, 1)
                sc.drain()

        @block.sync
        def _(sync):
            for grp in range(16):
                tc, oh = divmod(grp, 2)
                sync.wait_ge(s_sc, SC3 + grp + 1)
                sync.dma_start(out=ffn0_o[tc * 128:(tc + 1) * 128, oh * 2048:(oh + 1) * 2048],
                               in_=evb[grp % 2][:, 0:2048]).then_inc(s_st[grp % 2], 16)
            sync.dma_start(out=xnT_o.rearrange("d (b u) -> d b u", u=XC)[:, :, 1:XC],
                           in_=xnT_d.rearrange("d (b u) -> d b u", u=XC)[:, :, 1:XC]
                           ).then_inc(s_so, 16)

        @block.gpsimd
        def _(g):
            g.wait_ge(s_st[0], 1088 + 16 * 8)
            g.wait_ge(s_st[1], 1088 + 16 * 8)
            g.wait_ge(s_gt, 256)
            g.wait_ge(s_so, 32)

    return nc




def host_prep(ins, core):
    x0 = ins["L0_attn_hidden"]
    sl = slice(256 * core, 256 * core + 256)
    xa = np.ascontiguousarray(x0[:, sl].reshape(NTOK, D))
    if core == 0:
        xh = np.zeros((4, D), np.float32)
    else:
        xh = np.ascontiguousarray(x0[:, 256 * core - 1])
    d = dict(
        xa=xa, xh=xh,
        wkT=np.ascontiguousarray(ins["L0_attn_w_key"].T),
        wrT=np.ascontiguousarray(ins["L0_attn_w_receptance"].T),
        wgT=np.ascontiguousarray(ins["L0_attn_w_gate"].T),
        wvT=np.ascontiguousarray(ins["L0_attn_w_value"].T),
        tmk=np.ascontiguousarray(ins["L0_attn_time_mix_key"].reshape(32, 128).T),
        tmr=np.ascontiguousarray(ins["L0_attn_time_mix_receptance"].reshape(32, 128).T),
        tmg=np.ascontiguousarray(ins["L0_attn_time_mix_gate"].reshape(32, 128).T),
        tmv=np.ascontiguousarray(ins["L0_attn_time_mix_value"].reshape(32, 128).T),
        ApT=np.ascontiguousarray(ins["L0_ffn_A"].transpose(2, 0, 1).reshape(D, 1024)),
        Bp=np.ascontiguousarray(ins["L0_ffn_B"].transpose(0, 2, 1).reshape(1024, D)),
        WrT=np.ascontiguousarray(ins["L0_ffn_Wr"].T),
        idf=np.eye(128, dtype=np.float32),
        idb=np.eye(128, dtype=np.float32).astype(ml_dtypes.bfloat16),
    )
    return d


NP = 32

def epat_consts():
    i = np.arange(128, dtype=np.float32)
    pats = [np.mod(i, 8), i, -1.0 - np.mod(i, 8), 7.0 - np.mod(i, 8), -1.0 - i, 127.0 - i]
    E = np.stack([np.tile(p, (128, 1)) for p in pats], axis=1)
    return np.ascontiguousarray(E.reshape(128, 6 * 128))


def mask_consts():
    j = np.arange(128)[:, None]
    i = np.arange(128)[None, :]
    M0 = (i == j)
    M1 = (j // 8 == i // 8) & (j < i)
    M2 = (i // 8 == j // 8 + 1)
    M3 = (i // 8 >= j // 8 + 2)
    Mi = np.stack([M0, M1, M2], 0).astype(np.uint8)
    mi = np.ascontiguousarray(Mi.transpose(1, 0, 2).reshape(128, 3 * 128))
    mf = np.ascontiguousarray(M3.astype(np.float32))
    return mi, mf




def build_B1():
    nc = bass.Bass("TRN2", target_bir_lowering=False)
    kT = nc.dram_tensor("kT", [D, NTOK], BF16, kind="ExternalInput").ap()
    vT = nc.dram_tensor("vT", [D, NTOK], BF16, kind="ExternalInput").ap()
    decT = nc.dram_tensor("decT", [64, 64], F32, kind="ExternalInput").ap()
    EPAT = nc.dram_tensor("EPAT", [128, 768], F32, kind="ExternalInput").ap()
    idf = nc.dram_tensor("idf", [128, 128], F32, kind="ExternalInput").ap()
    SSUM = nc.dram_tensor("SSUM", [128, 8192], F32, kind="ExternalOutput").ap()

    with ExitStack() as ctx:
        E = ctx.enter_context
        SA = E(nc.sbuf_tensor("SA", [128, 8192], F32))
        LNW = E(nc.sbuf_tensor("LNW", [64, 64], F32))
        LNW2 = E(nc.sbuf_tensor("LNW2", [128, 32], F32))
        WL = E(nc.sbuf_tensor("WL", [128, 32], F32))
        WKH = E(nc.sbuf_tensor("WKH", [128, 128], F32))
        WMT = E(nc.sbuf_tensor("WMT", [128, 128], F32))
        EP = E(nc.sbuf_tensor("EP", [128, 768], F32))
        SID = E(nc.sbuf_tensor("SID", [128, 128], F32))
        KTB = [E(nc.sbuf_tensor(f"KTB{i}", [128, 128], BF16)) for i in range(2)]
        VB = [E(nc.sbuf_tensor(f"VB{i}", [128, 128], BF16)) for i in range(2)]
        KH = [E(nc.sbuf_tensor(f"KH{i}", [128, 128], BF16)) for i in range(2)]
        pD = [E(nc.psum_tensor(f"pD{i}", [128, 512], F32)) for i in range(2)]
        pT = E(nc.psum_tensor("pT", [128, 512], F32))

        s_ld = [E(nc.semaphore("s_ld0")), E(nc.semaphore("s_ld1"))]
        s_c = E(nc.semaphore("s_c"))
        s_t = E(nc.semaphore("s_t"))
        s_v = E(nc.semaphore("s_v"))
        s_sc = E(nc.semaphore("s_sc"))
        s_st = E(nc.semaphore("s_st"))
        block = E(nc.Block())

        def v_hp(hp): return 1 + hp * 18
        def v_wts(hp): return v_hp(hp) + 1
        def v_wmtE(hp): return v_hp(hp) + 2
        def v_kh(hp, j): return v_hp(hp) + 2 + 2 * j + 1
        def v_s(hp, j): return v_hp(hp) + 2 + 2 * j + 2
        def sc_hp(hp): return 1 + hp + 1
        def t_hp(hp): return hp * 9
        def t_wmt(hp): return t_hp(hp) + 1
        def t_ds(hp, j): return t_hp(hp) + 1 + j + 1

        @block.sync
        def _(sync):
            sync.dma_start(out=LNW[:], in_=decT).then_inc(s_c, 16)
            sync.dma_start(out=EP[:], in_=EPAT).then_inc(s_c, 16)
            sync.dma_start(out=SID[:], in_=idf).then_inc(s_c, 16)
            for it in range(NP * 8):
                hp, j = divmod(it, 8)
                b, cc = divmod(j, 2)
                par = it % 2
                if it >= 2:
                    sync.wait_ge(s_t, t_ds((it - 2) // 8, (it - 2) % 8))
                    sync.wait_ge(s_v, v_kh((it - 2) // 8, (it - 2) % 8))
                rows = slice(128 * hp, 128 * hp + 128)
                cols = slice(256 * b + 128 * cc, 256 * b + 128 * cc + 128)
                sync.dma_start(out=KTB[par][:], in_=kT[rows, cols], transpose=True
                               ).then_inc(s_ld[par], 16)
                sync.dma_start(out=VB[par][:], in_=vT[rows, cols], transpose=True
                               ).then_inc(s_ld[par], 16)
            sync.wait_ge(s_v, v_s(NP - 1, 7))
            sync.dma_start(out=SSUM, in_=SA[:]).then_inc(s_st, 16)
            sync.wait_ge(s_st, 16)

        @block.scalar
        def _(sc):
            sc.wait_ge(s_c, 48)
            sc.activation(LNW[:], LNW[:], AF.Exp).then_inc(s_sc, 1)
            for hp in range(NP):
                sc.wait_ge(s_v, v_wts(hp))
                sc.activation(WKH[:], WKH[:], AF.Exp)
                sc.drain()
                sc.activation(WL[:, hp:hp + 1], LNW2[:, hp:hp + 1], AF.Exp, scale=128.0
                              ).then_inc(s_sc, 1)

        @block.vector
        def _(v):
            v.wait_ge(s_c, 48)
            v.wait_ge(s_sc, 1)
            v.tensor_scalar(LNW[:], LNW[:], -1.0, None, op0=AL.mult)
            v.drain()
            lv = LNW[:].rearrange("d (hp t) -> d hp t", t=2)
            v.tensor_copy(LNW2[0:64, :], lv[:, :, 0])
            v.tensor_copy(LNW2[64:128, :], lv[:, :, 1])
            v.drain()
            v.memset(SA[:], 0.0)
            v.drain()
            v.tensor_copy(LNW2[0:64, 0:1], LNW2[0:64, 0:1]).then_inc(s_v, 1)
            v.drain()
            for hp in range(NP):
                if hp >= 1:
                    v.wait_ge(s_sc, sc_hp(hp - 1))
                    v.wait_ge(s_t, t_wmt(hp - 1))
                v.tensor_scalar(WKH[:], EP[:, 640:768], LNW2[:, hp:hp + 1], None,
                                op0=AL.mult).then_inc(s_v, 1)
                v.wait_ge(s_t, t_wmt(hp))
                v.tensor_copy(WMT[:], pT[:, 0:128]).then_inc(s_v, 1)
                v.drain()
                for j in range(8):
                    b, cc = divmod(j, 2)
                    it = hp * 8 + j
                    par = it % 2
                    v.wait_ge(s_ld[par], 16 * 2 * (it // 2 + 1))
                    v.tensor_tensor(KH[par][:], KTB[par][:], WMT[:], op=AL.mult
                                    ).then_inc(s_v, 1)
                    v.wait_ge(s_t, t_ds(hp, j))
                    col = (b * 32 + hp) * 64
                    if cc == 0:
                        v.tensor_copy(SA[0:64, col:col + 64], pD[0][0:64, 0:64])
                        v.tensor_copy(SA[64:128, col:col + 64], pD[1][0:64, 0:64]
                                      ).then_inc(s_v, 1)
                        v.drain()
                    else:
                        v.wait_ge(s_sc, sc_hp(hp))
                        v.tensor_scalar(SA[:, col:col + 64], SA[:, col:col + 64],
                                        WL[:, hp:hp + 1], None, op0=AL.mult)
                        v.drain()
                        v.tensor_tensor(SA[0:64, col:col + 64], SA[0:64, col:col + 64],
                                        pD[0][0:64, 0:64], op=AL.add)
                        v.tensor_tensor(SA[64:128, col:col + 64], SA[64:128, col:col + 64],
                                        pD[1][0:64, 0:64], op=AL.add).then_inc(s_v, 1)
                        v.drain()

        @block.tensor
        def _(t):
            for hp in range(NP):
                t.wait_ge(s_sc, sc_hp(hp))
                if hp >= 1:
                    t.wait_ge(s_v, v_wmtE(hp - 1))
                t.transpose(pT[:, 0:128], WKH[:], SID[:]).then_inc(s_t, 1)
                for j in range(8):
                    it = hp * 8 + j
                    par = it % 2
                    t.wait_ge(s_v, v_kh(hp, j))
                    if it >= 1:
                        t.wait_ge(s_v, v_s((it - 1) // 8, (it - 1) % 8))
                    t.matmul(pD[0][0:64, 0:64], KH[par][:, 0:64], VB[par][:, 0:64],
                             start=True, stop=True)
                    t.matmul(pD[1][0:64, 0:64], KH[par][:, 64:128], VB[par][:, 64:128],
                             start=True, stop=True).then_inc(s_t, 1)

    return nc




def build_B2():
    nc = bass.Bass("TRN2", target_bir_lowering=False)

    def di(name, shape, dt=BF16):
        return nc.dram_tensor(name, shape, dt, kind="ExternalInput").ap()

    kT = di("kT", [D, NTOK]); rT = di("rT", [D, NTOK]); vT = di("vT", [D, NTOK])
    gT = di("gT", [D, NTOK])
    ffn0 = di("ffn0", [NTOK, D])
    xa = di("xa", [NTOK, D], F32)
    SUMS = di("SUMS", [7 * 128, 8192], F32)
    decT = di("decT", [64, 64], F32)
    tfT = di("tfT", [64, 64], F32)
    EPAT = di("EPAT", [128, 768], F32)
    MASKI = nc.dram_tensor("MASKI", [128, 3 * 128], mybir.dt.uint8, kind="ExternalInput").ap()
    MASKF = di("MASKF", [128, 128], F32)
    gnwB = di("gnwB", [128, D], F32)
    gnbB = di("gnbB", [128, D], F32)
    w_outT = di("w_outT", [D, D], F32)
    idf = di("idf", [128, 128], F32)
    idb = di("idb", [128, 128], BF16)
    x1_o = nc.dram_tensor("x1_o", [NTOK, D], F32, kind="ExternalOutput").ap()
    o_dbg = nc.dram_tensor("o_dbg", [128, 8 * 4096], BF16, kind="ExternalOutput").ap()

    with ExitStack() as ctx:
        E = ctx.enter_context
        AB1 = E(nc.sbuf_tensor("AB1", [128, 32768], BF16))
        AB2 = E(nc.sbuf_tensor("AB2", [128, 32768], BF16))
        AB3 = E(nc.sbuf_tensor("AB3", [128, 16384], BF16))
        o_tok = AB1
        SA = AB2[:, 0:16384].bitcast(F32)
        ognT = AB2
        sumb = [AB3[:, 0:8192].bitcast(F32), AB3[:, 8192:16384].bitcast(F32)]
        gnw_s = AB3[:, 0:8192].bitcast(F32)
        gnb_s = AB3[:, 8192:16384].bitcast(F32)
        WOP = AB3[:, 0:16384]
        LNW = E(nc.sbuf_tensor("LNW", [64, 64], F32))
        TFT = E(nc.sbuf_tensor("TFT", [64, 64], F32))
        LNW2 = E(nc.sbuf_tensor("LNW2", [128, 32], F32))
        TF2 = E(nc.sbuf_tensor("TF2", [128, 32], F32))
        WL = E(nc.sbuf_tensor("WL", [128, 32], F32))
        W256 = E(nc.sbuf_tensor("W256", [128, 64], F32))
        WPW = E(nc.sbuf_tensor("WPW", [128, 6 * 128], F32))
        WMT = E(nc.sbuf_tensor("WMT", [128, 128], F32))
        EP = E(nc.sbuf_tensor("EP", [128, 768], F32))
        MSI = E(nc.sbuf_tensor("MSI", [128, 3 * 128], mybir.dt.uint8))
        MSF = E(nc.sbuf_tensor("MSF", [128, 128], F32))
        SID = E(nc.sbuf_tensor("SID", [128, 128], F32))
        SIDB = E(nc.sbuf_tensor("SIDB", [128, 128], BF16))
        KB = [E(nc.sbuf_tensor(f"KB{i}", [128, 128], BF16)) for i in range(2)]
        RB = [E(nc.sbuf_tensor(f"RB{i}", [128, 128], BF16)) for i in range(2)]
        VB = [E(nc.sbuf_tensor(f"VB{i}", [128, 128], BF16)) for i in range(2)]
        KTB = [E(nc.sbuf_tensor(f"KTB{i}", [128, 128], BF16)) for i in range(2)]
        KH = [E(nc.sbuf_tensor(f"KH{i}", [128, 128], BF16)) for i in range(2)]
        R8 = [E(nc.sbuf_tensor(f"R8{i}", [128, 128], BF16)) for i in range(2)]
        RF = [E(nc.sbuf_tensor(f"RF{i}", [128, 128], BF16)) for i in range(2)]
        KTF = [E(nc.sbuf_tensor(f"KTF{i}", [128, 128], BF16)) for i in range(2)]
        K0 = [E(nc.sbuf_tensor(f"K0{i}", [128, 128], BF16)) for i in range(2)]
        K1 = [E(nc.sbuf_tensor(f"K1{i}", [128, 128], BF16)) for i in range(2)]
        KF = [E(nc.sbuf_tensor(f"KF{i}", [128, 128], BF16)) for i in range(2)]
        AT = [E(nc.sbuf_tensor(f"AT{i}", [128, 128], BF16)) for i in range(2)]
        SBF = [E(nc.sbuf_tensor(f"SBF{i}", [128, 64], BF16)) for i in range(2)]
        GST = E(nc.sbuf_tensor("GST", [128, 320], F32))
        GTC = [E(nc.sbuf_tensor(f"GTC{i}", [128, 1024], BF16)) for i in range(2)]
        XAB = [E(nc.sbuf_tensor(f"XAB{i}", [128, 512], F32)) for i in range(2)]
        FFB = [E(nc.sbuf_tensor(f"FFB{i}", [128, 512], BF16)) for i in range(2)]
        X1B = [E(nc.sbuf_tensor(f"X1B{i}", [128, 512], F32)) for i in range(2)]
        SQB = E(nc.sbuf_tensor("SQB", [128, 4096], BF16))

        pP = [E(nc.psum_tensor(f"pP{i}", [128, 512], F32)) for i in range(4)]
        pO = [E(nc.psum_tensor(f"pO{i}", [128, 512], F32)) for i in range(2)]
        pD = [E(nc.psum_tensor(f"pD{i}", [128, 512], F32)) for i in range(2)]

        s_l = [E(nc.semaphore("s_l0")), E(nc.semaphore("s_l1"))]
        s_c = E(nc.semaphore("s_c"))
        s_m = [E(nc.semaphore("s_m0")), E(nc.semaphore("s_m1"))]
        s_t = E(nc.semaphore("s_t"))
        s_v = E(nc.semaphore("s_v"))
        s_g = E(nc.semaphore("s_g"))
        s_sc = E(nc.semaphore("s_sc"))
        s_st = [E(nc.semaphore("s_st0")), E(nc.semaphore("s_st1"))]
        s_so = E(nc.semaphore("s_so"))
        s_wp = E(nc.semaphore("s_wp"))
        s_bar = E(nc.semaphore("s_bar"))
        block = E(nc.Block())

        bar = [0]

        def barrier():
            bar[0] += 1
            tgt = 5 * bar[0]
            for dec in (block.sync, block.tensor, block.vector, block.scalar, block.gpsimd):
                @dec
                def _(e):
                    e.drain()
                    e.sem_inc(s_bar, 1)
                    e.wait_ge(s_bar, tgt)

        V_SET = 1
        V_PRE = V_SET + 14

        def v_hp(hp): return V_PRE + hp * (3 + 8 * 8)
        def v_wts(hp): return v_hp(hp) + 1
        def v_wcl(hp): return v_hp(hp) + 2
        def v_wmtE(hp): return v_hp(hp) + 3

        def v_it(hp, j, stage):
            b = v_hp(hp) + 3 + j * 8
            return b + {"prep": 1, "sbf": 2, "comb0": 3, "comb1": 4,
                        "evac0": 5, "supd0": 6, "evac1": 7, "supd1": 8}[stage]
        V_SCAN = v_hp(NP)

        SC_SET = 1
        SC_PRE = SC_SET + 7

        def sc_hp(hp): return SC_PRE + hp + 1
        SC_SCAN = sc_hp(NP - 1)

        def t_hp(hp): return hp * (1 + 8 * 6)
        def t_wmt(hp): return t_hp(hp) + 1

        def t_it(hp, j, stage):
            b = t_hp(hp) + 1 + j * 6
            return b + {"P0": 1, "P1": 2, "O0": 3, "D0": 4, "O1": 5, "D1": 6}[stage]
        T_SCAN = t_hp(NP)

        def g_it(hp, j): return hp * 8 + j + 1
        G_SCAN = NP * 8

        SM_B2 = [112, 112]
        SM_B3 = [SM_B2[0] + 256, SM_B2[1] + 256]

        @block.sync
        def _(sync):
            sync.dma_start(out=LNW[:], in_=decT).then_inc(s_c, 16)
            sync.dma_start(out=TFT[:], in_=tfT).then_inc(s_c, 16)
            sync.dma_start(out=EP[:], in_=EPAT).then_inc(s_c, 16)
            sync.dma_start(out=MSI[:], in_=MASKI).then_inc(s_c, 16)
            sync.dma_start(out=MSF[:], in_=MASKF).then_inc(s_c, 16)
            sync.dma_start(out=SID[:], in_=idf).then_inc(s_c, 16)
            sync.dma_start(out=SIDB[:], in_=idb).then_inc(s_c, 16)
            for m in range(7):
                for half in range(2):
                    q = 2 * m + half
                    if q >= 2:
                        sync.wait_ge(s_v, V_SET + (q - 2) + 1)
                    sync.dma_start(out=sumb[q % 2][:, 0:4096],
                                   in_=SUMS[m * 128:(m + 1) * 128,
                                            half * 4096:(half + 1) * 4096]
                                   ).then_inc(s_m[q % 2], 16)
            for it in range(NP * 8):
                hp, j = divmod(it, 8)
                b, cc = divmod(j, 2)
                par = it % 2
                if it >= 2:
                    hp2, j2 = divmod(it - 2, 8)
                    sync.wait_ge(s_t, t_it(hp2, j2, "D1"))
                    sync.wait_ge(s_v, v_it(hp2, j2, "prep"))
                    sync.wait_ge(s_g, g_it(hp2, j2))
                rows = slice(128 * hp, 128 * hp + 128)
                cols = slice(256 * b + 128 * cc, 256 * b + 128 * cc + 128)
                sync.dma_start(out=KB[par][:], in_=kT[rows, cols]).then_inc(s_l[par], 16)
                sync.dma_start(out=RB[par][:], in_=rT[rows, cols]).then_inc(s_l[par], 16)
                sync.dma_start(out=VB[par][:], in_=vT[rows, cols], transpose=True
                               ).then_inc(s_l[par], 16)
                sync.dma_start(out=KTB[par][:], in_=kT[rows, cols], transpose=True
                               ).then_inc(s_l[par], 16)

        @block.scalar
        def _(sc):
            sc.wait_ge(s_c, 112)
            sc.activation(LNW[:], LNW[:], AF.Exp).then_inc(s_sc, 1)
            sc.wait_ge(s_v, V_SET)
            for m in range(7):
                if m >= 2:
                    sc.wait_ge(s_v, V_SET + 2 * (m - 2) + 2)
                sc.activation(W256[:, 32 * (m % 2):32 * (m % 2) + 32], LNW2[:], AF.Exp,
                              scale=256.0 * m).then_inc(s_sc, 1)
            for hp in range(NP):
                sc.wait_ge(s_v, v_wts(hp))
                sc.activation(WPW[:], WPW[:], AF.Exp)
                sc.drain()
                sc.activation(WL[:, hp:hp + 1], LNW2[:, hp:hp + 1], AF.Exp, scale=128.0
                              ).then_inc(s_sc, 1)

        @block.vector
        def _(v):
            v.wait_ge(s_c, 112)
            v.wait_ge(s_sc, SC_SET)
            v.tensor_scalar(LNW[:], LNW[:], -1.0, None, op0=AL.mult)
            v.drain()
            lv = LNW[:].rearrange("d (hp t) -> d hp t", t=2)
            tv = TFT[:].rearrange("d (hp t) -> d hp t", t=2)
            v.tensor_copy(LNW2[0:64, :], lv[:, :, 0])
            v.tensor_copy(LNW2[64:128, :], lv[:, :, 1])
            v.tensor_copy(TF2[0:64, :], tv[:, :, 0])
            v.tensor_copy(TF2[64:128, :], tv[:, :, 1])
            v.drain()
            v.memset(SA[:, :], 0.0)
            v.memset(GST[:, 0:1], 1e-5)
            v.drain()
            v.tensor_copy(GST[:, 1:2], GST[:, 0:1]).then_inc(s_v, 1)
            v.drain()
            for m in range(7):
                v.wait_ge(s_sc, SC_SET + m + 1)
                for half in range(2):
                    q = 2 * m + half
                    v.wait_ge(s_m[q % 2], 16 * (q // 2 + 1))
                    sv = sumb[q % 2][:, 0:4096].rearrange("p (b hp e) -> p b hp e", b=2, e=64)
                    av = SA[:, half * 4096:(half + 1) * 4096].rearrange(
                        "p (b hp e) -> p b hp e", b=2, e=64)
                    for hp in range(NP):
                        ins = v.scalar_tensor_tensor(av[:, :, hp, :], sv[:, :, hp, :],
                                                     W256[:, 32 * (m % 2) + hp:32 * (m % 2) + hp + 1],
                                                     av[:, :, hp, :], op0=AL.mult, op1=AL.add)
                        if hp == NP - 1:
                            ins.then_inc(s_v, 1)
                    v.drain()
            for hp in range(NP):
                if hp >= 1:
                    v.wait_ge(s_g, g_it(hp - 1, 7))
                    v.wait_ge(s_sc, sc_hp(hp - 1))
                    v.wait_ge(s_t, t_wmt(hp - 1))
                v.tensor_scalar(WPW[:], EP[:], LNW2[:, hp:hp + 1], None, op0=AL.mult)
                v.drain()
                v.tensor_scalar(WPW[:, 256:384], WPW[:, 256:384], 69.0, None, op0=AL.min)
                v.drain()
                v.tensor_scalar(WPW[:, 512:640], WPW[:, 512:640], 69.0, None, op0=AL.min
                                ).then_inc(s_v, 1)
                v.wait_ge(s_sc, sc_hp(hp))
                v.tensor_copy(WL[0:64, 0:1], WL[0:64, 0:1]).then_inc(s_v, 1)
                v.drain()
                v.wait_ge(s_t, t_wmt(hp))
                v.tensor_copy(WMT[:], pP[0][:, 0:128]).then_inc(s_v, 1)
                v.drain()
                for j in range(8):
                    b, cc = divmod(j, 2)
                    it = hp * 8 + j
                    par = it % 2
                    v.wait_ge(s_l[par], 16 * 4 * (it // 2 + 1))
                    v.tensor_tensor(R8[par][:], RB[par][:], WPW[:, 0:128], op=AL.mult)
                    v.tensor_tensor(RF[par][:], RB[par][:], WPW[:, 128:256], op=AL.mult)
                    v.tensor_scalar(KTF[par][:], KB[par][:], TF2[:, hp:hp + 1], None, op0=AL.mult)
                    v.tensor_tensor(KH[par][:], KTB[par][:], WMT[:], op=AL.mult
                                    ).then_inc(s_v, 1)
                    col = (b * 32 + hp) * 64
                    if it >= 2:
                        hp2, j2 = divmod(it - 2, 8)
                        v.wait_ge(s_t, t_it(hp2, j2, "O1"))
                    v.tensor_copy(SBF[par][0:64, :], SA[0:64, col:col + 64])
                    v.tensor_copy(SBF[par][64:128, :], SA[64:128, col:col + 64]).then_inc(s_v, 1)
                    v.drain()
                    for hi in range(2):
                        v.wait_ge(s_t, t_it(hp, j, f"P{hi}"))
                        a = AT[hi]
                        v.tensor_tensor(a[:], pP[3][:, 0:128], MSF[:], op=AL.mult)
                        v.drain()
                        v.copy_predicated(a[:], MSI[:, 128:256], pP[1][:, 0:128])
                        v.drain()
                        v.copy_predicated(a[:], MSI[:, 256:384], pP[2][:, 0:128])
                        v.drain()
                        v.copy_predicated(a[:], MSI[:, 0:128], pP[0][:, 0:128])
                        v.drain()
                        v.tensor_copy(a[:, 0:1], a[:, 0:1]).then_inc(s_v, 1)
                    for hi in range(2):
                        v.wait_ge(s_t, t_it(hp, j, f"O{hi}"))
                        tile = b * 2 + cc
                        h = 2 * hp + hi
                        v.tensor_copy(o_tok[:, tile * 4096 + 64 * h: tile * 4096 + 64 * h + 64],
                                      pO[hi][:, 0:64]).then_inc(s_v, 1)
                        v.wait_ge(s_t, t_it(hp, j, f"D{hi}"))
                        rowsl = slice(64 * hi, 64 * hi + 64)
                        v.tensor_scalar(SA[rowsl, col:col + 64], SA[rowsl, col:col + 64],
                                        WL[rowsl, hp:hp + 1], None, op0=AL.mult)
                        v.drain()
                        v.tensor_tensor(SA[rowsl, col:col + 64], SA[rowsl, col:col + 64],
                                        pD[hi][0:64, 0:64], op=AL.add).then_inc(s_v, 1)
                        v.drain()

        @block.gpsimd
        def _(g):
            for hp in range(NP):
                g.wait_ge(s_v, v_wcl(hp))
                for j in range(8):
                    it = hp * 8 + j
                    par = it % 2
                    g.wait_ge(s_l[par], 16 * 4 * (it // 2 + 1))
                    g.tensor_tensor(K0[par][:], KB[par][:], WPW[:, 256:384], op=AL.mult)
                    g.tensor_tensor(K1[par][:], KB[par][:], WPW[:, 384:512], op=AL.mult)
                    g.tensor_tensor(KF[par][:], KB[par][:], WPW[:, 512:640], op=AL.mult
                                    ).then_inc(s_g, 1)
                    g.drain()

        @block.tensor
        def _(t):
            for hp in range(NP):
                t.wait_ge(s_v, v_wcl(hp))
                t.wait_ge(s_sc, sc_hp(hp))
                if hp >= 1:
                    t.wait_ge(s_v, v_it(hp - 1, 7, "comb1"))
                t.transpose(pP[0][:, 0:128], WPW[:, 640:768], SID[:]).then_inc(s_t, 1)
                for j in range(8):
                    it = hp * 8 + j
                    par = it % 2
                    t.wait_ge(s_v, v_it(hp, j, "prep"))
                    t.wait_ge(s_g, g_it(hp, j))
                    if j == 0:
                        t.wait_ge(s_v, v_wmtE(hp))
                    for hi in range(2):
                        rows = slice(64 * hi, 64 * hi + 64)
                        if hi == 1:
                            t.wait_ge(s_v, v_it(hp, j, "comb0"))
                        elif it >= 1:
                            hp2, j2 = divmod(it - 1, 8)
                            t.wait_ge(s_v, v_it(hp2, j2, "comb1"))
                        t.matmul(pP[0][:, 0:128], KTF[par][rows, :], RB[par][rows, :],
                                 start=True, stop=True)
                        t.matmul(pP[1][:, 0:128], K0[par][rows, :], R8[par][rows, :],
                                 start=True, stop=True)
                        t.matmul(pP[2][:, 0:128], K1[par][rows, :], R8[par][rows, :],
                                 start=True, stop=True)
                        t.matmul(pP[3][:, 0:128], KF[par][rows, :], RF[par][rows, :],
                                 start=True, stop=True).then_inc(s_t, 1)
                    for hi in range(2):
                        rows = slice(64 * hi, 64 * hi + 64)
                        t.wait_ge(s_v, v_it(hp, j, f"comb{hi}"))
                        t.wait_ge(s_v, v_it(hp, j, "sbf"))
                        if it >= 1:
                            hp2, j2 = divmod(it - 1, 8)
                            t.wait_ge(s_v, v_it(hp2, j2, f"evac{hi}"))
                        t.matmul(pO[hi][:, 0:64], AT[hi][:], VB[par][:, 64 * hi:64 * hi + 64],
                                 start=True, stop=False)
                        t.matmul(pO[hi][:, 0:64], RF[par][rows, :], SBF[par][rows, :],
                                 start=False, stop=True).then_inc(s_t, 1)
                        if it >= 1:
                            hp2, j2 = divmod(it - 1, 8)
                            t.wait_ge(s_v, v_it(hp2, j2, f"supd{hi}"))
                        t.matmul(pD[hi][0:64, 0:64], KH[par][:, 64 * hi:64 * hi + 64],
                                 VB[par][:, 64 * hi:64 * hi + 64],
                                 start=True, stop=True).then_inc(s_t, 1)

        barrier()

        g_mu = GST[:, 8:72]
        g_ms = GST[:, 72:136]
        g_var = GST[:, 136:200]
        g_rstd = GST[:, 200:264]
        g_ln = GST[:, 256:320]
        eps = GST[:, 0:1]

        @block.sync
        def _(sync):
            sync.dma_start(out=gnw_s[:, 0:4096], in_=gnwB).then_inc(s_c, 16)
            sync.dma_start(out=gnb_s[:, 0:4096], in_=gnbB).then_inc(s_c, 16)

        @block.vector
        def _(v):
            v.wait_ge(s_c, 112 + 32)
            for i in range(8):
                ov = o_tok[:, i * 4096:(i + 1) * 4096]
                o3 = ov.rearrange("p (h e) -> p h e", e=64)
                v.tensor_tensor(SQB[:], ov, ov, op=AL.mult)
                v.drain()
                v.tensor_reduce(g_ms[:], SQB[:].rearrange("p (h e) -> p h e", e=64),
                                axis=AX.X, op=AL.add)
                v.tensor_reduce(g_mu[:], o3, axis=AX.X, op=AL.add)
                v.drain()
                v.tensor_scalar(g_mu[:], g_mu[:], 1.0 / 64, None, op0=AL.mult)
                v.tensor_scalar(g_ms[:], g_ms[:], 1.0 / 64, None, op0=AL.mult)
                v.drain()
                v.tensor_tensor(g_var[:], g_mu[:], g_mu[:], op=AL.mult)
                v.drain()
                v.tensor_tensor(g_var[:], g_ms[:], g_var[:], op=AL.subtract).then_inc(s_v, 1)
                v.wait_ge(s_sc, SC_SCAN + i + 1)
                mub = g_mu[:].rearrange("p (h o) -> p h o", o=1).broadcast_to([128, 64, 64])
                rb = g_rstd[:].rearrange("p (h o) -> p h o", o=1).broadcast_to([128, 64, 64])
                v.tensor_tensor(o3, o3, mub, op=AL.subtract)
                v.drain()
                v.tensor_tensor(o3, o3, rb, op=AL.mult)
                v.drain()
                v.tensor_tensor(ov, ov, gnw_s[:, 0:4096], op=AL.mult)
                v.drain()
                v.tensor_tensor(ov, ov, gnb_s[:, 0:4096], op=AL.add).then_inc(s_v, 1)
                v.drain()

        @block.scalar
        def _(sc):
            for i in range(8):
                sc.wait_ge(s_v, V_SCAN + 2 * i + 1)
                sc.activation(g_ln[:], g_var[:], AF.Ln, bias=eps)
                sc.drain()
                sc.activation(g_rstd[:], g_ln[:], AF.Exp, scale=-0.5).then_inc(s_sc, 1)

        V_GN = V_SCAN + 16
        SC_GN = SC_SCAN + 8

        barrier()

        def v_ev(n): return V_GN + 9 * (n // 8) + (n % 8) + 1
        def v_gm(f): return V_GN + 9 * (f + 1)
        V_TP = V_GN + 9 * 32

        @block.sync
        def _(sync):
            sync.dma_start(out=o_dbg, in_=o_tok[:]).then_inc(s_so, 16)
            for f in range(32):
                if f >= 2:
                    sync.wait_ge(s_v, v_gm(f - 2))
                sync.dma_start(out=GTC[f % 2][:], in_=gT[f * 128:(f + 1) * 128, :]
                               ).then_inc(s_m[f % 2], 16)

        @block.tensor
        def _(t):
            for f in range(32):
                for i in range(8):
                    n = f * 8 + i
                    if n >= 2:
                        t.wait_ge(s_v, v_ev(n - 2))
                    pt = pP[n % 2]
                    t.transpose(pt[:, 0:64].bitcast(BF16),
                                o_tok[:, i * 4096 + f * 128: i * 4096 + (f + 1) * 128],
                                SIDB[:]).then_inc(s_t, 1)

        @block.vector
        def _(v):
            for f in range(32):
                for i in range(8):
                    n = f * 8 + i
                    v.wait_ge(s_t, T_SCAN + n + 1)
                    pt = pP[n % 2]
                    v.tensor_copy(ognT[:, f * 1024 + i * 128: f * 1024 + (i + 1) * 128],
                                  pt[:, 0:64].bitcast(BF16)).then_inc(s_v, 1)
                v.drain()
                v.wait_ge(s_m[f % 2], SM_B2[f % 2] + 16 * (f // 2 + 1))
                v.tensor_tensor(ognT[:, f * 1024:(f + 1) * 1024],
                                ognT[:, f * 1024:(f + 1) * 1024], GTC[f % 2][:], op=AL.mult
                                ).then_inc(s_v, 1)
                v.drain()

        T_TP = T_SCAN + 256

        barrier()

        @block.gpsimd
        def _(g):
            for pc in range(8):
                if pc >= 1:
                    g.wait_ge(s_t, T_TP + 8 * pc)
                src = w_outT[:, pc * 512:(pc + 1) * 512].rearrange("(fg p) c -> p fg c", p=128)
                wv = WOP.rearrange("p (f c) -> p f c", c=512)
                g.dma_start(out=wv[:, 0:16, :], in_=src[:, 0:16, :]).then_inc(s_wp, 16)
                g.dma_start(out=wv[:, 16:32, :], in_=src[:, 16:32, :]).then_inc(s_wp, 16)

        @block.sync
        def _(sync):
            for pc in range(8):
                for tc in range(8):
                    n = pc * 8 + tc
                    par = n % 2
                    if n >= 2:
                        sync.wait_ge(s_v, V_TP + (n - 2) + 1)
                    sync.dma_start(out=XAB[par][:],
                                   in_=xa[tc * 128:(tc + 1) * 128, pc * 512:(pc + 1) * 512]
                                   ).then_inc(s_l[par], 16)
                    sync.dma_start(out=FFB[par][:],
                                   in_=ffn0[tc * 128:(tc + 1) * 128, pc * 512:(pc + 1) * 512]
                                   ).then_inc(s_l[par], 16)
                    if n >= 1:
                        m = n - 1
                        mpar = m % 2
                        sync.wait_ge(s_v, V_TP + m + 1)
                        tc2, pc2 = m % 8, m // 8
                        sync.dma_start(out=x1_o[tc2 * 128:(tc2 + 1) * 128,
                                                pc2 * 512:(pc2 + 1) * 512],
                                       in_=X1B[mpar][:]).then_inc(s_st[mpar], 16)
            m = 63
            sync.wait_ge(s_v, V_TP + m + 1)
            sync.dma_start(out=x1_o[(m % 8) * 128:(m % 8 + 1) * 128,
                                    (m // 8) * 512:(m // 8 + 1) * 512],
                           in_=X1B[m % 2][:]).then_inc(s_st[m % 2], 16)

        @block.tensor
        def _(t):
            t.wait_ge(s_v, V_TP)
            for pc in range(8):
                t.wait_ge(s_wp, 32 * (pc + 1))
                for tc in range(8):
                    n = pc * 8 + tc
                    if n >= 2:
                        t.wait_ge(s_v, V_TP + (n - 2) + 1)
                    wv = WOP.rearrange("p (f c) -> p f c", c=512)
                    ins = None
                    for f in range(32):
                        ins = t.matmul(pP[n % 2][:, 0:512],
                                       ognT[:, f * 1024 + tc * 128: f * 1024 + (tc + 1) * 128],
                                       wv[:, f, :], start=(f == 0), stop=(f == 31))
                    ins.then_inc(s_t, 1)

        @block.vector
        def _(v):
            for pc in range(8):
                for tc in range(8):
                    n = pc * 8 + tc
                    par = n % 2
                    v.wait_ge(s_t, T_TP + n + 1)
                    v.wait_ge(s_l[par], 16 * 4 * 128 + 32 * (n // 2 + 1))
                    if n >= 2:
                        v.wait_ge(s_st[par], 16 * (n // 2))
                    v.tensor_tensor(X1B[par][:], pP[n % 2][:, 0:512], XAB[par][:], op=AL.add)
                    v.drain()
                    v.tensor_tensor(X1B[par][:], X1B[par][:], FFB[par][:], op=AL.add
                                    ).then_inc(s_v, 1)
                    v.drain()

        @block.gpsimd
        def _(g):
            g.wait_ge(s_st[0], 16 * 32)
            g.wait_ge(s_st[1], 16 * 32)
            g.wait_ge(s_so, 16)

    return nc




def host_prep_B1(ins, core, a_res):
    return dict(
        kT=a_res["kT_o"], vT=a_res["vT_o"],
        decT=np.ascontiguousarray(ins["L0_attn_time_decay"].T),
        EPAT=epat_consts(), idf=np.eye(128, dtype=np.float32),
    )


def host_prep_B2(ins, core, a_res, ssums):
    x0 = ins["L0_attn_hidden"]
    sl = slice(256 * core, 256 * core + 256)
    xa = np.ascontiguousarray(x0[:, sl].reshape(NTOK, D))
    SUMS = np.zeros((7 * 128, 8192), np.float32)
    for m in range(7):
        j = core - 1 - m
        if j >= 0:
            SUMS[m * 128:(m + 1) * 128] = ssums[j]
    gnw = ins["L0_attn_gn_weight"].astype(np.float32)
    gnb = ins["L0_attn_gn_bias"].astype(np.float32)
    return dict(
        kT=a_res["kT_o"], rT=a_res["rT_o"], vT=a_res["vT_o"], gT=a_res["gT_o"],
        ffn0=a_res["ffn0_o"], xa=xa, SUMS=SUMS,
        decT=np.ascontiguousarray(ins["L0_attn_time_decay"].T),
        tfT=np.ascontiguousarray(ins["L0_attn_time_faaaa"].T),
        EPAT=epat_consts(), MASKI=mask_consts()[0], MASKF=mask_consts()[1],
        gnwB=np.ascontiguousarray(np.broadcast_to(gnw, (128, D))),
        gnbB=np.ascontiguousarray(np.broadcast_to(gnb, (128, D))),
        w_outT=np.ascontiguousarray(ins["L0_attn_w_output"].T),
        idf=np.eye(128, dtype=np.float32),
        idb=np.eye(128, dtype=np.float32).astype(ml_dtypes.bfloat16),
    )

def build_C():
    nc = bass.Bass("TRN2", target_bir_lowering=False)

    def di(name, shape, dt=F32):
        return nc.dram_tensor(name, shape, dt, kind="ExternalInput").ap()

    xa = di("xa", [NTOK, D])
    wdT = di("wdT", [D, 128])
    bdn = di("bdn", [128, 1])
    wuT = di("wuT", [128, D])
    bup = di("bup", [1, D])
    gate1 = di("gate1", [1, 1])
    inwT = di("inwT", [128, 552])
    cw = di("cw", [288, 7])
    cb = di("cb", [288, 1])
    dtb = di("dtb", [4, 1])
    alog = di("alog", [4, 1])
    fcT = di("fcT", [256, 4])
    dDc = di("dDc", [4, 1])
    owT = di("owT", [256, 128])
    nwB = di("nwB", [128, 256])
    idf = di("idf", [128, 128])
    onesf = di("onesf", [1, 128])
    idb = nc.dram_tensor("idb", [128, 128], BF16, kind="ExternalInput").ap()

    xn1T_o = nc.dram_tensor("xn1T_o", [D, NTOK], BF16, kind="ExternalOutput").ap()
    xn1tok_o = nc.dram_tensor("xn1tok_o", [NTOK, D], BF16, kind="ExternalOutput").ap()
    attn1_o = nc.dram_tensor("attn1_o", [NTOK, D], BF16, kind="ExternalOutput").ap()
    hy_dbg = nc.dram_tensor("hy_dbg", [128, NTOK], F32, kind="ExternalOutput").ap()

    xn1T_d = nc.dram_tensor("xn1T_d", [D, NTOK], BF16).ap()

    with ExitStack() as ctx:
        E = ctx.enter_context
        ARC = E(nc.sbuf_tensor("ARC", [128, 40960], BF16))
        XAT = [ARC[:, 0:8192].bitcast(F32), ARC[:, 8192:16384].bitcast(F32)]
        SQV = ARC[:, 16384:24576].bitcast(F32)
        STG = [ARC[:, 24576:28672], ARC[:, 28672:32768]]
        XNT = ARC[:, 32768:36864][:, 0:4096]
        SMC = E(nc.sbuf_tensor("SMC", [128, 32], F32))
        XNC = [E(nc.sbuf_tensor(f"XNC{i}", [128, NTOK], BF16)) for i in range(2)]
        WDT = E(nc.sbuf_tensor("WDT", [128, NF * 128], BF16))
        WUP = E(nc.sbuf_tensor("WUP", [128, D], BF16))
        INW = E(nc.sbuf_tensor("INW", [128, 552], BF16))
        FCTs = E(nc.sbuf_tensor("FCTs", [128, 8], F32))
        OWT = E(nc.sbuf_tensor("OWT", [128, 256], BF16))
        CW3 = E(nc.sbuf_tensor("CW3", [128, 21], F32))
        CBS = E(nc.sbuf_tensor("CBS", [128, 3], F32))
        SC1 = E(nc.sbuf_tensor("SC1", [128, 32], F32))
        NWB = E(nc.sbuf_tensor("NWB", [128, 256], F32))
        SID = E(nc.sbuf_tensor("SID", [128, 128], F32))
        SIDB = E(nc.sbuf_tensor("SIDB", [128, 128], BF16))
        ONEF = E(nc.sbuf_tensor("ONEF", [1, 128], F32))
        ONEBB = E(nc.sbuf_tensor("ONEBB", [1, 128], BF16))
        BUPR = E(nc.sbuf_tensor("BUPR", [1, D], BF16))
        HMID = E(nc.sbuf_tensor("HMID", [128, NTOK], F32))
        HMIDB = E(nc.sbuf_tensor("HMIDB", [128, NTOK], BF16))
        HYO = E(nc.sbuf_tensor("HYO", [128, NTOK], F32))
        ZXT = E(nc.sbuf_tensor("ZXT", [128, 2 * NTOK], F32))
        XB0 = E(nc.sbuf_tensor("XB0", [128, NTOK], F32))
        XB1 = E(nc.sbuf_tensor("XB1", [128, NTOK], F32))
        SMALLF = E(nc.sbuf_tensor("SMALLF", [4, 8192], F32))
        SMALL2 = E(nc.sbuf_tensor("SMALL2", [32, 2048], F32))
        XB2 = SMALL2[0:32, 0:1024]
        DTRF = SMALLF[0:4, 0:1024]
        DTRB = SMALLF[0:4, 1024:2048]
        XC0 = E(nc.sbuf_tensor("XC0", [128, NTOK], F32))
        XC1 = E(nc.sbuf_tensor("XC1", [128, NTOK], F32))
        XC2 = SMALL2[0:32, 1024:2048]
        DTF = SMALLF[0:4, 2048:3072]
        DTB2 = SMALLF[0:4, 3072:4096]
        DAF = SMALLF[0:4, 4096:5120]
        DAB = SMALLF[0:4, 5120:6144]
        T4A = SMALLF[0:4, 6144:7168]
        T4B = SMALLF[0:4, 7168:8192]
        XSK = ARC[:, 36864:40960].bitcast(F32)
        BCK = E(nc.sbuf_tensor("BCK", [128, 256], F32))
        DTK = E(nc.sbuf_tensor("DTK", [128, 32], F32))
        DAK = E(nc.sbuf_tensor("DAK", [128, 32], F32))
        DTKB = E(nc.sbuf_tensor("DTKB", [128, 32], F32))
        DAKB = E(nc.sbuf_tensor("DAKB", [128, 32], F32))
        FCK = E(nc.sbuf_tensor("FCK", [128, 32], F32))
        HST = ARC[:, 0:8192].bitcast(F32)
        TMPB = ARC[:, 8192:16384].bitcast(F32)
        DBX = E(nc.sbuf_tensor("DBX", [128, 256], F32))
        YFW = ARC[:, 16384:20480].bitcast(F32)
        YBW = ARC[:, 20480:24576].bitcast(F32)
        YC = ARC[:, 24576:28672].bitcast(F32)
        ZSL = ARC[:, 28672:32768].bitcast(F32)
        ZSK = ARC[:, 32768:36864].bitcast(F32)
        YKB = E(nc.sbuf_tensor("YKB", [128, 2048], BF16))
        YNT = E(nc.sbuf_tensor("YNT", [128, 2 * NTOK], BF16))
        FCF = E(nc.sbuf_tensor("FCF", [4, NTOK], F32))
        RSD = E(nc.sbuf_tensor("RSD", [128, 32], F32))
        HCB = E(nc.sbuf_tensor("HCB", [128, NTOK], BF16))
        EVC = [ARC[:, 16384:18432], ARC[:, 18432:20480]]
        XNK = [ARC[:, 20480:22528], ARC[:, 22528:24576]]

        pA = E(nc.psum_tensor("pA", [128, 2048], F32))
        pB = E(nc.psum_tensor("pB", [128, 1024], F32))
        pT0 = E(nc.psum_tensor("pT0", [128, 1024], BF16))
        pT1 = E(nc.psum_tensor("pT1", [128, 1024], BF16))

        s_l = [E(nc.semaphore("s_l0")), E(nc.semaphore("s_l1"))]
        s_c = E(nc.semaphore("s_c"))
        s_w = E(nc.semaphore("s_w"))
        s_t = E(nc.semaphore("s_t"))
        s_v = E(nc.semaphore("s_v"))
        s_sc = E(nc.semaphore("s_sc"))
        s_st = [E(nc.semaphore("s_st0")), E(nc.semaphore("s_st1"))]
        s_so = E(nc.semaphore("s_so"))
        s_bar = E(nc.semaphore("s_bar"))
        block = E(nc.Block())

        bar = [0]

        def barrier():
            bar[0] += 1
            tgt = 5 * bar[0]
            for dec in (block.sync, block.tensor, block.vector, block.scalar, block.gpsimd):
                @dec
                def _(e):
                    e.drain()
                    e.sem_inc(s_bar, 1)
                    e.wait_ge(s_bar, tgt)

        C = dict(t=0, v=0, sc=0)

        def v_var(i): return 34 * i + 1
        def v_norm(i): return 34 * i + 2
        def v_evac(i, fc): return 34 * i + 3 + fc
        def v_flush(i): return 34 * i + 34
        V1 = 34 * NT
        T1 = 32 * NT
        SC1n = NT

        @block.sync
        def _(sync):
            sync.dma_start(out=SID[:], in_=idf).then_inc(s_c, 16)
            sync.dma_start(out=SIDB[:], in_=idb).then_inc(s_c, 16)
            sync.dma_start(out=ONEF[:], in_=onesf).then_inc(s_c, 16)
            cwv = CW3[:].rearrange("p (c t) -> p c t", t=7)
            sync.dma_start(out=cwv[0:128, 0:1, :],
                           in_=cw[0:128, :].rearrange("p (o t) -> p o t", o=1)).then_inc(s_c, 16)
            sync.dma_start(out=cwv[0:128, 1:2, :],
                           in_=cw[128:256, :].rearrange("p (o t) -> p o t", o=1)).then_inc(s_c, 16)
            sync.dma_start(out=cwv[0:32, 2:3, :],
                           in_=cw[256:288, :].rearrange("p (o t) -> p o t", o=1)).then_inc(s_c, 16)
            sync.dma_start(out=CBS[0:128, 0:1], in_=cb[0:128, :]).then_inc(s_c, 16)
            sync.dma_start(out=CBS[0:128, 1:2], in_=cb[128:256, :]).then_inc(s_c, 16)
            sync.dma_start(out=CBS[0:32, 2:3], in_=cb[256:288, :]).then_inc(s_c, 16)
            sync.dma_start(out=SC1[0:4, 0:1], in_=dtb).then_inc(s_c, 16)
            sync.dma_start(out=SC1[0:4, 1:2], in_=alog).then_inc(s_c, 16)
            sync.dma_start(out=SC1[0:4, 2:3], in_=dDc).then_inc(s_c, 16)
            sync.dma_start(out=SC1[0:1, 3:4], in_=gate1).then_inc(s_c, 16)
            sync.dma_start(out=SC1[0:128, 16:17], in_=bdn).then_inc(s_c, 16)
            sync.dma_start(out=NWB[:], in_=nwB).then_inc(s_c, 16)
            for i in range(NT + 1):
                if 2 <= i < NT:
                    sync.wait_ge(s_v, v_norm(i - 2))
                if i < NT:
                    sync.dma_start(out=XAT[i % 2][:], in_=xa[i * 128:(i + 1) * 128, :]
                                   ).then_inc(s_l[i % 2], 16)
                j = i - 1
                if 0 <= j < NT:
                    sync.wait_ge(s_v, v_norm(j))
                    sync.dma_start(out=xn1tok_o[j * 128:(j + 1) * 128, :], in_=XNT[:]
                                   ).then_inc(s_so, 16)
                    sync.wait_ge(s_v, v_flush(j))
                    dst = xn1T_d[:, j * 128:(j + 1) * 128].rearrange("(fc p) c -> p fc c", p=128)
                    sync.dma_start(out=dst,
                                   in_=STG[j % 2].rearrange("p (fc c) -> p fc c", c=128)
                                   ).then_inc(s_st[j % 2], 16)
            sync.wait_ge(s_st[0], 16 * 4)
            sync.wait_ge(s_st[1], 16 * 4)
            sync.dma_start(out=xn1T_o, in_=xn1T_d).then_inc(s_so, 16)

        @block.gpsimd
        def _(g):
            g.dma_start(out=WDT[:].rearrange("p (f c) -> p f c", c=128),
                        in_=wdT.rearrange("(f p) c -> p f c", p=128)).then_inc(s_w, 16)
            g.dma_start(out=WUP[:], in_=wuT).then_inc(s_w, 16)
            g.dma_start(out=INW[:], in_=inwT).then_inc(s_w, 16)
            g.dma_start(out=BUPR[:], in_=bup).then_inc(s_w, 16)
            g.dma_start(out=FCTs[:].rearrange("p (c o) -> p c o", o=4),
                        in_=fcT.rearrange("(c p) o -> p c o", p=128)).then_inc(s_w, 16)
            g.dma_start(out=OWT[:].rearrange("p (c q) -> p c q", q=128),
                        in_=owT.rearrange("(c p) q -> p c q", p=128)).then_inc(s_w, 16)
            g.dma_start(out=ONEBB[:], in_=onesf).then_inc(s_w, 16)

        @block.scalar
        def _(sc):
            for i in range(NT):
                sc.wait_ge(s_v, v_var(i))
                sc.activation(SMC[:, 6:7], SMC[:, 4:5], AF.Ln, bias=SMC[:, 8:9])
                sc.drain()
                sc.activation(SMC[:, 5:6], SMC[:, 6:7], AF.Exp, scale=-0.5).then_inc(s_sc, 1)

        @block.vector
        def _(v):
            v.wait_ge(s_c, 240)
            v.memset(SMC[:, 8:9], 1e-5)
            v.drain()
            for i in range(NT):
                v.wait_ge(s_l[i % 2], 16 * (i // 2 + 1))
                buf = XAT[i % 2]
                v.tensor_tensor(SQV[:], buf[:], buf[:], op=AL.mult)
                v.drain()
                v.tensor_reduce(SMC[:, 1:2], SQV[:], axis=AX.X, op=AL.add)
                v.tensor_reduce(SMC[:, 0:1], buf[:], axis=AX.X, op=AL.add)
                v.drain()
                v.tensor_scalar(SMC[:, 2:3], SMC[:, 0:1], 1.0 / D, None, op0=AL.mult)
                v.drain()
                v.tensor_tensor(SMC[:, 6:7], SMC[:, 2:3], SMC[:, 2:3], op=AL.mult)
                v.tensor_scalar(SMC[:, 4:5], SMC[:, 1:2], 1.0 / D, None, op0=AL.mult)
                v.drain()
                v.tensor_tensor(SMC[:, 4:5], SMC[:, 4:5], SMC[:, 6:7], op=AL.subtract
                                ).then_inc(s_v, 1)
                if i >= 1:
                    v.wait_ge(s_t, 32 * i)
                    v.wait_ge(s_so, 16 * i)
                v.wait_ge(s_sc, i + 1)
                v.tensor_scalar(XNT[:], buf[:], SMC[:, 2:3], SMC[:, 5:6],
                                op0=AL.subtract, op1=AL.mult).then_inc(s_v, 1)
                if 2 <= i:
                    v.wait_ge(s_st[i % 2], 16 * (i // 2))
                for fc in range(NF):
                    v.wait_ge(s_t, 32 * i + fc + 1)
                    pt = (pT0, pT1)[fc % 2]
                    v.tensor_copy(STG[i % 2][:, fc * 128:(fc + 1) * 128], pt[:, 0:128]
                                  ).then_inc(s_v, 1)
                v.drain()

        @block.tensor
        def _(t):
            for i in range(NT):
                t.wait_ge(s_v, v_norm(i))
                for fc in range(NF):
                    g = 32 * i + fc
                    if g >= 2:
                        i2, fc2 = divmod(g - 2, 32)
                        t.wait_ge(s_v, v_evac(i2, fc2))
                    pt = (pT0, pT1)[fc % 2]
                    t.transpose(pt[:, 0:128], XNT[:, fc * 128:(fc + 1) * 128], SIDB[:]
                                ).then_inc(s_t, 1)

        C["t"] = T1
        C["v"] = V1
        C["sc"] = SC1n
        barrier()

        @block.scalar
        def _(sc):
            sc.activation(SC1[0:4, 17:18], SC1[0:4, 1:2], AF.Exp)
            sc.drain()
            sc.activation(SC1[0:1, 19:20], SC1[0:1, 3:4], AF.Sigmoid).then_inc(s_sc, 1)
            C["sc"] += 1

        @block.vector
        def _(v):
            v.wait_ge(s_sc, C["sc"])
            v.tensor_scalar(SC1[0:4, 17:18], SC1[0:4, 17:18], -1.0, None, op0=AL.mult)
            v.memset(SC1[:, 18:19], 1.0)
            v.memset(RSD[:, 0:4], 0.0)
            v.drain()

        barrier()

        @block.sync
        def _(sync):
            for f in range(NF):
                if f >= 2:
                    sync.wait_ge(s_t, C["t"] + f - 1)
                sync.dma_start(out=XNC[f % 2][:], in_=xn1T_d[f * 128:(f + 1) * 128, :]
                               ).then_inc(s_l[f % 2], 16)

        @block.tensor
        def _(t):
            t.wait_ge(s_c, 240)
            t.wait_ge(s_w, 112)
            for f in range(NF):
                t.wait_ge(s_l[f % 2], 16 * (4 + f // 2 + 1))
                ins = None
                for h in range(2):
                    ins = t.matmul(pA[:, h * 1024:h * 1024 + 512],
                                   WDT[:, f * 128:(f + 1) * 128],
                                   XNC[f % 2][:, h * 512:(h + 1) * 512],
                                   start=(f == 0), stop=(f == NF - 1))
                ins.then_inc(s_t, 1)
            C["t"] += NF

        @block.scalar
        def _(sc):
            sc.wait_ge(s_t, C["t"])
            sc.activation(HMID[:, 0:512], pA[:, 0:512], AF.Gelu, bias=SC1[:, 16:17])
            sc.drain()
            sc.activation(HMID[:, 512:1024], pA[:, 1024:1536], AF.Gelu, bias=SC1[:, 16:17]
                          ).then_inc(s_sc, 1)
            C["sc"] += 1

        @block.vector
        def _(v):
            v.wait_ge(s_sc, C["sc"])
            v.tensor_copy(HMIDB[:], HMID[:]).then_inc(s_v, 1)
            C["v"] += 1
            v.drain()

        barrier()

        def hydra(dbg_store, wait_so=None):
            chunks = [("z0", 0, 128, ZXT, 0), ("z1", 128, 256, ZXT, 1024),
                      ("xb0", 256, 384, XB0, 0), ("xb1", 384, 512, XB1, 0),
                      ("xb2", 512, 544, XB2, 0), ("dtf", 544, 548, DTRF, 0),
                      ("dtb", 548, 552, DTRB, 0)]

            @block.tensor
            def _(t):
                for ci, (nm, c0, c1, dst, off) in enumerate(chunks):
                    M = c1 - c0
                    bank = pA if ci % 2 == 0 else pB
                    if ci >= 2:
                        t.wait_ge(s_v, C["v"] + ci - 1)
                    for h in range(2):
                        t.matmul(bank[0:M, h * 512:(h + 1) * 512], INW[:, c0:c1],
                                 HMIDB[:, h * 512:(h + 1) * 512],
                                 start=True, stop=True).then_inc(s_t, 1)

            @block.vector
            def _(v):
                for ci, (nm, c0, c1, dst, off) in enumerate(chunks):
                    M = c1 - c0
                    bank = pA if ci % 2 == 0 else pB
                    v.wait_ge(s_t, C["t"] + 2 * ci + 2)
                    v.tensor_copy(dst[0:M, off:off + 1024], bank[0:M, 0:1024]).then_inc(s_v, 1)
                    v.drain()
            C["t"] += 14
            C["v"] += 7
            barrier()

            @block.vector
            def _(v):
                for ci, (xin, xout, w, bcol, M) in enumerate(
                        [(XB0, XC0, 0, 0, 128), (XB1, XC1, 1, 1, 128), (XB2, XC2, 2, 2, 32)]):
                    for b in range(4):
                        for j in range(b + 1):
                            wcol = CW3[0:M, w * 7 + 6 - b + j: w * 7 + 6 - b + j + 1]
                            src = xin[0:M, j * 256:(j + 1) * 256]
                            dstv = xout[0:M, b * 256:(b + 1) * 256]
                            if j == 0:
                                v.tensor_scalar(dstv, src, wcol, None, op0=AL.mult)
                            else:
                                v.scalar_tensor_tensor(dstv, src, wcol, dstv,
                                                       op0=AL.mult, op1=AL.add)
                            v.drain()
                v.tensor_copy(RSD[0:1, 0:1], RSD[0:1, 0:1]).then_inc(s_v, 1)
                v.drain()
                for dtr, tb in [(DTRF, T4A), (DTRB, T4B)]:
                    v.tensor_scalar(tb[0:4, :], dtr[0:4, :], SC1[0:4, 0:1], 0.0,
                                    op0=AL.add, op1=AL.max)
                v.drain()
                v.tensor_copy(RSD[0:1, 1:2], RSD[0:1, 0:1]).then_inc(s_v, 1)
                v.drain()
                v.wait_ge(s_sc, C["sc"] + 3)
                v.tensor_tensor(DTF[0:4, :], DTF[0:4, :], T4A[0:4, :], op=AL.add)
                v.tensor_tensor(DTB2[0:4, :], DTB2[0:4, :], T4B[0:4, :], op=AL.add)
                v.drain()
                v.tensor_scalar(DAF[0:4, :], DTF[0:4, :], SC1[0:4, 17:18], None, op0=AL.mult)
                v.tensor_scalar(DAB[0:4, :], DTB2[0:4, :], SC1[0:4, 17:18], None, op0=AL.mult
                                ).then_inc(s_v, 1)
                v.drain()

            @block.scalar
            def _(sc):
                sc.wait_ge(s_so, 144)
                sc.wait_ge(s_st[0], 16 * 4)
                sc.wait_ge(s_st[1], 16 * 4)
                sc.wait_ge(s_v, C["v"] + 1)
                sc.activation(XC0[:], XC0[:], AF.Silu, bias=CBS[:, 0:1])
                sc.drain()
                sc.activation(XC1[:], XC1[:], AF.Silu, bias=CBS[:, 1:2])
                sc.drain()
                sc.activation(XC2[0:32, :], XC2[0:32, :], AF.Silu, bias=CBS[0:32, 2:3]
                              ).then_inc(s_sc, 1)
                sc.activation(ZSL[:, 0:1024], ZXT[:, 0:1024], AF.Silu)
                sc.drain()
                sc.activation(ZSL[:, 1024:2048], ZXT[:, 1024:2048], AF.Silu)
                sc.drain()
                sc.wait_ge(s_v, C["v"] + 2)
                for dtr, dst in [(DTRF, DTF), (DTRB, DTB2)]:
                    sc.activation(dst[0:4, :], dtr[0:4, :], AF.Abs, bias=SC1[0:4, 0:1])
                    sc.drain()
                    sc.activation(dst[0:4, :], dst[0:4, :], AF.Exp, scale=-1.0)
                    sc.drain()
                    sc.activation(dst[0:4, :], dst[0:4, :], AF.Ln, bias=SC1[0:4, 18:19]
                                  ).then_inc(s_sc, 1)
                    sc.drain()
                sc.wait_ge(s_v, C["v"] + 3)
                sc.activation(DAF[0:4, :], DAF[0:4, :], AF.Exp)
                sc.drain()
                sc.activation(DAB[0:4, :], DAB[0:4, :], AF.Exp).then_inc(s_sc, 1)
            C["v"] += 3
            C["sc"] += 4
            barrier()

            @block.tensor
            def _(t):
                for h in range(2):
                    t.matmul(pB[0:4, h * 512:(h + 1) * 512], FCTs[:, 0:4],
                             XC0[:, h * 512:(h + 1) * 512], start=True, stop=False)
                    t.matmul(pB[0:4, h * 512:(h + 1) * 512], FCTs[:, 4:8],
                             XC1[:, h * 512:(h + 1) * 512], start=False, stop=True
                             ).then_inc(s_t, 1)
                nix = [0]

                def tp(src_ap, wid, base=0):
                    pt = (pT0, pT1)[nix[0] % 2]
                    if nix[0] >= 2:
                        t.wait_ge(s_v, C["v"] + nix[0])
                    M = src_ap.shape[0]
                    t.transpose(pt[:, 0:2 * wid].bitcast(F32),
                                src_ap, SID[base:base + M, base:base + M]).then_inc(s_t, 1)
                    nix[0] += 1
                for st in range(2):
                    for l in range(4):
                        cols = slice(l * 256 + st * 128, l * 256 + st * 128 + 128)
                        tp(XC0[:, cols], 128)
                        tp(XC1[:, cols], 128)
                        tp(XC2[0:32, cols], 32)
                        tp(DTF[0:4, cols], 4)
                        tp(DAF[0:4, cols], 4)
                        tp(DTB2[0:4, cols], 4)
                        tp(DAB[0:4, cols], 4)
                        tp(FCF[0:4, cols], 4)
                        tp(ZSL[:, 0 * 1024 + l * 256 + st * 128: 0 * 1024 + l * 256 + st * 128 + 128], 128)
                        tp(ZSL[:, 1 * 1024 + l * 256 + st * 128: 1 * 1024 + l * 256 + st * 128 + 128], 128)

            @block.vector
            def _(v):
                v.wait_ge(s_so, 144)
                v.wait_ge(s_t, C["t"] + 2)
                v.tensor_scalar(FCF[0:4, :], pB[0:4, 0:1024], SC1[0:4, 2:3], None, op0=AL.add
                                ).then_inc(s_v, 1)
                v.drain()
                n = 0
                for st in range(2):
                    for l in range(4):
                        tl = st * 4 + l
                        dsts = [
                            (XSK[:, st * 1024 + l * 256 + 0:st * 1024 + l * 256 + 128], 128),
                            (XSK[:, st * 1024 + l * 256 + 128:st * 1024 + l * 256 + 256], 128),
                            (BCK[:, tl * 32:(tl + 1) * 32], 32),
                            (DTK[:, tl * 4:(tl + 1) * 4], 4),
                            (DAK[:, tl * 4:(tl + 1) * 4], 4),
                            (DTKB[:, tl * 4:(tl + 1) * 4], 4),
                            (DAKB[:, tl * 4:(tl + 1) * 4], 4),
                            (FCK[:, tl * 4:(tl + 1) * 4], 4),
                            (ZSK[:, st * 1024 + l * 256 + 0:st * 1024 + l * 256 + 128], 128),
                            (ZSK[:, st * 1024 + l * 256 + 128:st * 1024 + l * 256 + 256], 128),
                        ]
                        for dst, wid in dsts:
                            pt = (pT0, pT1)[n % 2]
                            v.wait_ge(s_t, C["t"] + 2 + n + 1)
                            v.tensor_copy(dst, pt[:, 0:2 * wid].bitcast(F32)).then_inc(s_v, 1)
                            v.drain()
                            n += 1
            C["t"] += 2 + 80
            C["v"] += 1 + 80
            barrier()

            @block.vector
            def _(v):
                for dirn in range(2):
                    dtk = DTK if dirn == 0 else DTKB
                    dak = DAK if dirn == 0 else DAKB
                    yout = YFW if dirn == 0 else YBW
                    for st in range(2):
                        v.memset(HST[:], 0.0)
                        v.drain()
                        for tau in range(4):
                            l = tau if dirn == 0 else 3 - tau
                            tl = st * 4 + l
                            xsv = XSK[:, st * 1024 + l * 256: st * 1024 + (l + 1) * 256]
                            dtv = dtk[:, tl * 4:(tl + 1) * 4]
                            dav = dak[:, tl * 4:(tl + 1) * 4]
                            Bv = BCK[:, tl * 32: tl * 32 + 16]
                            Cv = BCK[:, tl * 32 + 16: tl * 32 + 32]
                            dtb_v = dtv.rearrange("p (n o) -> p n o", o=1).broadcast_to([128, 4, 64])
                            v.tensor_tensor(DBX[:].rearrange("p (n h) -> p n h", n=4),
                                            xsv.rearrange("p (n h) -> p n h", n=4),
                                            dtb_v, op=AL.mult)
                            v.drain()
                            dab_v = dav.rearrange("p (n o) -> p n o", o=1).broadcast_to([128, 4, 1024])
                            v.tensor_tensor(HST[:].rearrange("p (n x) -> p n x", n=4),
                                            HST[:].rearrange("p (n x) -> p n x", n=4),
                                            dab_v, op=AL.mult)
                            v.drain()
                            dbx_v = DBX[:].rearrange("p (f o) -> p f o", o=1).broadcast_to([128, 256, 16])
                            bb_v = Bv.rearrange("p (o d) -> p o d", o=1).broadcast_to([128, 256, 16])
                            v.tensor_tensor(TMPB[:].rearrange("p (f d) -> p f d", d=16),
                                            dbx_v, bb_v, op=AL.mult)
                            v.drain()
                            v.tensor_tensor(HST[:], HST[:], TMPB[:], op=AL.add)
                            v.drain()
                            cc_v = Cv.rearrange("p (o d) -> p o d", o=1).broadcast_to([128, 256, 16])
                            v.tensor_tensor(TMPB[:].rearrange("p (f d) -> p f d", d=16),
                                            HST[:].rearrange("p (f d) -> p f d", d=16),
                                            cc_v, op=AL.mult)
                            v.drain()
                            v.tensor_reduce(yout[:, st * 1024 + l * 256: st * 1024 + (l + 1) * 256],
                                            TMPB[:].rearrange("p (f d) -> p f d", d=16),
                                            axis=AX.X, op=AL.add)
                            v.drain()
                v.tensor_copy(RSD[0:1, 2:3], RSD[0:1, 0:1]).then_inc(s_v, 1)
                v.drain()
            C["v"] += 1
            barrier()

            @block.vector
            def _(v):
                for st in range(2):
                    for l in range(4):
                        dst = YC[:, st * 1024 + l * 256: st * 1024 + (l + 1) * 256]
                        if l == 0:
                            v.tensor_copy(dst, YBW[:, st * 1024 + 1 * 256: st * 1024 + 2 * 256])
                        elif l == 3:
                            v.tensor_copy(dst, YFW[:, st * 1024 + 2 * 256: st * 1024 + 3 * 256])
                        else:
                            v.tensor_tensor(dst, YFW[:, st * 1024 + (l - 1) * 256: st * 1024 + l * 256],
                                            YBW[:, st * 1024 + (l + 1) * 256: st * 1024 + (l + 2) * 256],
                                            op=AL.add)
                v.drain()
                for st in range(2):
                    for l in range(4):
                        tl = st * 4 + l
                        xsv = XSK[:, st * 1024 + l * 256: st * 1024 + (l + 1) * 256]
                        fcv = FCK[:, tl * 4:(tl + 1) * 4].rearrange("p (n o) -> p n o", o=1
                                                                    ).broadcast_to([128, 4, 64])
                        v.tensor_tensor(TMPB[:, 0:256].rearrange("p (n h) -> p n h", n=4),
                                        xsv.rearrange("p (n h) -> p n h", n=4), fcv, op=AL.mult)
                        v.drain()
                        v.tensor_tensor(YC[:, st * 1024 + l * 256: st * 1024 + (l + 1) * 256],
                                        YC[:, st * 1024 + l * 256: st * 1024 + (l + 1) * 256],
                                        TMPB[:, 0:256], op=AL.add)
                        v.drain()
                v.tensor_tensor(YC[:], YC[:], ZSK[:], op=AL.mult)
                v.drain()
                v.tensor_tensor(TMPB[:, 0:2048], YC[:], YC[:], op=AL.mult)
                v.drain()
                v.tensor_reduce(RSD[:, 8:16], TMPB[:, 0:2048].rearrange("p (q f) -> p q f", f=256),
                                axis=AX.X, op=AL.add)
                v.drain()
                v.tensor_scalar(RSD[:, 8:16], RSD[:, 8:16], 1.0 / 256, 1e-5,
                                op0=AL.mult, op1=AL.add).then_inc(s_v, 1)
                v.drain()
                v.wait_ge(s_sc, C["sc"] + 1)
                rstd_v = RSD[:, 16:24].rearrange("p (q o) -> p q o", o=1).broadcast_to([128, 8, 256])
                v.tensor_tensor(YC[:].rearrange("p (q f) -> p q f", f=256),
                                YC[:].rearrange("p (q f) -> p q f", f=256), rstd_v, op=AL.mult)
                v.drain()
                nw_v = NWB[:].rearrange("p (o f) -> p o f", o=1).broadcast_to([128, 8, 256])
                v.tensor_tensor(YKB[:].rearrange("p (q f) -> p q f", f=256),
                                YC[:].rearrange("p (q f) -> p q f", f=256), nw_v, op=AL.mult
                                ).then_inc(s_v, 1)
                v.drain()
                for n in range(16):
                    pt = (pT0, pT1)[n % 2]
                    v.wait_ge(s_t, C["t"] + n + 1)
                    st, rest = divmod(n, 8)
                    l, chunk = divmod(rest, 2)
                    v.tensor_copy(YNT[:, chunk * 1024 + l * 256 + st * 128:
                                      chunk * 1024 + l * 256 + st * 128 + 128],
                                  pt[:, 0:128]).then_inc(s_v, 1)
                    v.drain()
                v.wait_ge(s_t, C["t"] + 16 + 2)
                if wait_so is not None:
                    v.wait_ge(s_so, wait_so)
                v.tensor_copy(HYO[:], pA[:, 0:1024])
                v.drain()
                v.tensor_copy(HMIDB[:], HYO[:]).then_inc(s_v, 1)
                v.drain()

            @block.scalar
            def _(sc):
                sc.wait_ge(s_v, C["v"] + 1)
                sc.activation(RSD[:, 24:32], RSD[:, 8:16], AF.Ln)
                sc.drain()
                sc.activation(RSD[:, 16:24], RSD[:, 24:32], AF.Exp, scale=-0.5).then_inc(s_sc, 1)

            @block.tensor
            def _(t):
                t.wait_ge(s_v, C["v"] + 2)
                for n in range(16):
                    st, rest = divmod(n, 8)
                    l, chunk = divmod(rest, 2)
                    pt = (pT0, pT1)[n % 2]
                    if n >= 2:
                        t.wait_ge(s_v, C["v"] + 2 + n - 1)
                    src = YKB[:, st * 1024 + l * 256 + chunk * 128:
                              st * 1024 + l * 256 + chunk * 128 + 128]
                    t.transpose(pt[:, 0:128], src, SIDB[:]).then_inc(s_t, 1)
                t.wait_ge(s_v, C["v"] + 2 + 16)
                for h in range(2):
                    t.matmul(pA[:, h * 512:(h + 1) * 512], OWT[:, 0:128],
                             YNT[:, 0 * 1024 + h * 512: 0 * 1024 + (h + 1) * 512],
                             start=True, stop=False)
                    t.matmul(pA[:, h * 512:(h + 1) * 512], OWT[:, 128:256],
                             YNT[:, 1 * 1024 + h * 512: 1 * 1024 + (h + 1) * 512],
                             start=False, stop=True).then_inc(s_t, 1)

            C["t"] += 16 + 2
            C["sc"] += 1
            C["v"] += 2 + 16 + 1
            if dbg_store:
                @block.sync
                def _(sync):
                    sync.wait_ge(s_v, C["v"])
                    sync.dma_start(out=hy_dbg, in_=HYO[:]).then_inc(s_so, 16)
            barrier()

        hydra(dbg_store=True)
        hydra(dbg_store=False, wait_so=160)

        @block.tensor
        def _(t):
            t.matmul(pB[:, 0:1], ONEF[:], SC1[0:1, 19:20], start=True, stop=True
                     ).then_inc(s_t, 1)
        C["t"] += 1

        @block.vector
        def _(v):
            v.wait_ge(s_t, C["t"])
            v.tensor_copy(SC1[:, 19:20], pB[:, 0:1])
            v.drain()
            v.tensor_scalar(SC1[:, 20:21], SC1[:, 19:20], -1.0, 1.0, op0=AL.mult, op1=AL.add)
            v.tensor_scalar(SC1[:, 21:22], SC1[:, 19:20], 1.0, None, op0=AL.add)
            v.drain()
            v.tensor_scalar(TMPB[:, 0:1024], HYO[:], SC1[:, 20:21], None, op0=AL.mult)
            v.drain()
            v.scalar_tensor_tensor(HCB[:], HMID[:], SC1[:, 21:22], TMPB[:, 0:1024],
                                   op0=AL.mult, op1=AL.add).then_inc(s_v, 1)
            v.drain()
        C["v"] += 1
        barrier()

        @block.sync
        def _(sync):
            sync.wait_ge(s_so, 144)
            for grp in range(16):
                tc, oh = divmod(grp, 2)
                par = grp % 2
                if grp >= 2:
                    sync.wait_ge(s_v, C["v"] + grp - 1)
                sync.dma_start(out=XNK[par][:],
                               in_=xn1tok_o[tc * 128:(tc + 1) * 128, oh * 2048:(oh + 1) * 2048]
                               ).then_inc(s_l[par], 16)
                if grp >= 1:
                    m = grp - 1
                    tc2, oh2 = divmod(m, 2)
                    sync.wait_ge(s_v, C["v"] + m + 1)
                    sync.dma_start(out=attn1_o[tc2 * 128:(tc2 + 1) * 128,
                                               oh2 * 2048:(oh2 + 1) * 2048],
                                   in_=EVC[m % 2][:]).then_inc(s_st[m % 2], 16)
            m = 15
            sync.wait_ge(s_v, C["v"] + m + 1)
            tc2, oh2 = divmod(m, 2)
            sync.dma_start(out=attn1_o[tc2 * 128:(tc2 + 1) * 128, oh2 * 2048:(oh2 + 1) * 2048],
                           in_=EVC[m % 2][:]).then_inc(s_st[m % 2], 16)

        @block.tensor
        def _(t):
            for grp in range(16):
                tc, oh = divmod(grp, 2)
                if grp >= 1:
                    t.wait_ge(s_v, C["v"] + grp)
                for j in range(4):
                    t.matmul(pA[:, j * 512:(j + 1) * 512], HCB[:, tc * 128:(tc + 1) * 128],
                             WUP[:, oh * 2048 + j * 512: oh * 2048 + (j + 1) * 512],
                             start=True, stop=False)
                    t.matmul(pA[:, j * 512:(j + 1) * 512], ONEBB[:],
                             BUPR[:, oh * 2048 + j * 512: oh * 2048 + (j + 1) * 512],
                             start=False, stop=True).then_inc(s_t, 1)
        @block.vector
        def _(v):
            base_l = [8, 8]
            for grp in range(16):
                tc, oh = divmod(grp, 2)
                par = grp % 2
                v.wait_ge(s_t, C["t"] + 4 * (grp + 1))
                v.wait_ge(s_l[par], 16 * (20 + grp // 2 + 1))
                if grp >= 2:
                    v.wait_ge(s_st[par], 16 * (4 + grp // 2))
                v.tensor_tensor(EVC[par][:], pA[:, 0:2048], XNK[par][:], op=AL.add
                                ).then_inc(s_v, 1)
                v.drain()
        C["t"] += 64
        C["v"] += 16

        @block.gpsimd
        def _(g):
            g.wait_ge(s_st[0], 16 * (4 + 8))
            g.wait_ge(s_st[1], 16 * (4 + 8))
            g.wait_ge(s_so, 16 * (NT + 2))

    return nc


def host_prep_C(ins, core, x1):
    return dict(
        xa=np.ascontiguousarray(x1),
        wdT=np.ascontiguousarray(ins["L1_attn_w_down"].T),
        bdn=np.ascontiguousarray(ins["L1_attn_b_down"].reshape(128, 1)),
        wuT=np.ascontiguousarray(ins["L1_attn_w_up"].T),
        bup=np.ascontiguousarray(ins["L1_attn_b_up"].reshape(1, 4096)),
        gate1=np.ascontiguousarray(ins["L1_attn_gate1"].reshape(1, 1)),
        inwT=np.ascontiguousarray(ins["L1_attn_hy_in_w"].T),
        cw=np.ascontiguousarray(ins["L1_attn_hy_conv_w"].reshape(288, 7)),
        cb=np.ascontiguousarray(ins["L1_attn_hy_conv_b"].reshape(288, 1)),
        dtb=np.ascontiguousarray(ins["L1_attn_hy_dt_bias"].reshape(4, 1)),
        alog=np.ascontiguousarray(ins["L1_attn_hy_A_log"].reshape(4, 1)),
        fcT=np.ascontiguousarray(ins["L1_attn_hy_fcD_w"].T),
        dDc=np.ascontiguousarray(ins["L1_attn_hy_D"].reshape(4, 1)),
        owT=np.ascontiguousarray(ins["L1_attn_hy_out_w"].T),
        nwB=np.ascontiguousarray(np.broadcast_to(ins["L1_attn_hy_norm_w"], (128, 256))),
        idf=np.eye(128, dtype=np.float32),
        onesf=np.ones((1, 128), np.float32),
        idb=np.eye(128, dtype=np.float32).astype(ml_dtypes.bfloat16),
    )


INTER = 11008
NIT = 86
NFD = 32

def build_D1():
    nc = bass.Bass("TRN2", target_bir_lowering=False)
    qg = nc.dram_tensor("qg", [INTER, 512], I32, kind="ExternalInput").ap()
    qu = nc.dram_tensor("qu", [INTER, 512], I32, kind="ExternalInput").ap()
    qd = nc.dram_tensor("qd", [D, 1376], I32, kind="ExternalInput").ap()
    sg = nc.dram_tensor("sg", [INTER, 32], F32, kind="ExternalInput").ap()
    su = nc.dram_tensor("su", [INTER, 32], F32, kind="ExternalInput").ap()
    sd = nc.dram_tensor("sd", [D, 86], F32, kind="ExternalInput").ap()
    zg = nc.dram_tensor("zg", [INTER, 32], I32, kind="ExternalInput").ap()
    zu = nc.dram_tensor("zu", [INTER, 32], I32, kind="ExternalInput").ap()
    zd = nc.dram_tensor("zd", [D, 86], I32, kind="ExternalInput").ap()
    rmk = nc.dram_tensor("rmk", [128, 8], mybir.dt.uint8, kind="ExternalInput").ap()

    Wg_o = nc.dram_tensor("Wg_o", [INTER, D], BF16, kind="ExternalOutput").ap()
    Wu_o = nc.dram_tensor("Wu_o", [INTER, D], BF16, kind="ExternalOutput").ap()
    Wd_o = nc.dram_tensor("Wd_o", [D, INTER], BF16, kind="ExternalOutput").ap()

    with ExitStack() as ctx:
        E = ctx.enter_context
        QB = [E(nc.sbuf_tensor(f"QB{i}", [128, 1376], I32)) for i in range(2)]
        SB = [E(nc.sbuf_tensor(f"SB{i}", [128, 86], F32)) for i in range(2)]
        ZB = [E(nc.sbuf_tensor(f"ZB{i}", [128, 86], I32)) for i in range(2)]
        ZS = E(nc.sbuf_tensor("ZS", [128, 86], F32))
        ZF = E(nc.sbuf_tensor("ZF", [128, 86], F32))
        Z4 = E(nc.sbuf_tensor("Z4", [128, 86], I32))
        ZT = E(nc.sbuf_tensor("ZT", [128, 86], I32))
        RMK = E(nc.sbuf_tensor("RMK", [128, 8], mybir.dt.uint8))
        W4 = E(nc.sbuf_tensor("W4", [128, INTER], U16))
        WB = [E(nc.sbuf_tensor(f"WB{i}", [128, INTER], BF16)) for i in range(2)]

        s_l = [E(nc.semaphore("s_l0")), E(nc.semaphore("s_l1"))]
        s_c = E(nc.semaphore("s_c"))
        s_v = E(nc.semaphore("s_v"))
        s_st = [E(nc.semaphore("s_st0")), E(nc.semaphore("s_st1"))]
        block = E(nc.Block())

        tiles = [("g", i) for i in range(NIT)] + [("u", i) for i in range(NIT)] + \
                [("d", i) for i in range(NFD)]
        QSRC = {"g": (qg, sg, zg, 512, 32, Wg_o, D), "u": (qu, su, zu, 512, 32, Wu_o, D),
                "d": (qd, sd, zd, 1376, 86, Wd_o, INTER)}

        @block.sync
        def _(sync):
            sync.dma_start(out=RMK[:], in_=rmk).then_inc(s_c, 16)
            for n, (mat, ti) in enumerate(tiles):
                q, ss, zz, PK, NG, WO, KW = QSRC[mat]
                par = n % 2
                if n >= 2:
                    sync.wait_ge(s_v, (n - 2) + 1)
                rows = slice(ti * 128, (ti + 1) * 128)
                sync.dma_start(out=QB[par][:, 0:PK], in_=q[rows, :]).then_inc(s_l[par], 16)
                sync.dma_start(out=SB[par][:, 0:NG], in_=ss[rows, :]).then_inc(s_l[par], 16)
                sync.dma_start(out=ZB[par][:, 0:NG], in_=zz[rows, :]).then_inc(s_l[par], 16)
                if n >= 1:
                    m = n - 1
                    mat2, ti2 = tiles[m]
                    WO2, KW2 = QSRC[mat2][5], QSRC[mat2][6]
                    sync.wait_ge(s_v, m + 1)
                    sync.dma_start(out=WO2[ti2 * 128:(ti2 + 1) * 128, :],
                                   in_=WB[m % 2][:, 0:KW2]).then_inc(s_st[m % 2], 16)
            m = len(tiles) - 1
            mat2, ti2 = tiles[m]
            WO2, KW2 = QSRC[mat2][5], QSRC[mat2][6]
            sync.wait_ge(s_v, m + 1)
            sync.dma_start(out=WO2[ti2 * 128:(ti2 + 1) * 128, :],
                           in_=WB[m % 2][:, 0:KW2]).then_inc(s_st[m % 2], 16)
            sync.wait_ge(s_st[0], 16 * 102)
            sync.wait_ge(s_st[1], 16 * 102)

        @block.vector
        def _(v):
            v.wait_ge(s_c, 16)
            for n, (mat, ti) in enumerate(tiles):
                q, ss, zz, PK, NG, WO, KW = QSRC[mat]
                par = n % 2
                v.wait_ge(s_l[par], 16 * 3 * (n // 2 + 1))
                if n >= 2:
                    v.wait_ge(s_st[par], 16 * (n // 2))
                for i8 in range(8):
                    v.tensor_scalar(ZT[:, 0:NG], ZB[par][:, 0:NG], 4 * i8, 15,
                                    op0=AL.logical_shift_right, op1=AL.bitwise_and)
                    v.drain()
                    v.copy_predicated(Z4[:, 0:NG], RMK[:, i8:i8 + 1].broadcast_to([128, NG]),
                                      ZT[:, 0:NG])
                    v.drain()
                v.tensor_scalar(ZF[:, 0:NG], Z4[:, 0:NG], 1.0, None, op0=AL.add)
                v.drain()
                v.tensor_tensor(ZS[:, 0:NG], ZF[:, 0:NG], SB[par][:, 0:NG], op=AL.mult)
                v.drain()
                qu16 = QB[par][:, 0:PK].bitcast(U16)
                KW2 = 2 * PK
                for i4 in range(4):
                    v.tensor_scalar(W4[:, i4 * KW2:(i4 + 1) * KW2], qu16, 4 * i4, 15,
                                    op0=AL.logical_shift_right, op1=AL.bitwise_and)
                v.drain()
                w4v = W4[:, 0:4 * KW2].rearrange("p (i m) -> p m i", i=4)
                for g in range(NG):
                    ins = v.tensor_scalar(
                        WB[par][:, g * 128:(g + 1) * 128].rearrange("p (m i) -> p m i", i=4),
                        w4v[:, g * 32:(g + 1) * 32, :],
                        SB[par][:, g:g + 1], ZS[:, g:g + 1],
                        op0=AL.mult, op1=AL.subtract)
                    if g == NG - 1:
                        ins.then_inc(s_v, 1)
                v.drain()

    return nc


def build_D2():
    nc = bass.Bass("TRN2", target_bir_lowering=False)
    Wg = nc.dram_tensor("Wg", [INTER, D], BF16, kind="ExternalInput").ap()
    Wu = nc.dram_tensor("Wu", [INTER, D], BF16, kind="ExternalInput").ap()
    Wd = nc.dram_tensor("Wd", [D, INTER], BF16, kind="ExternalInput").ap()
    xn1T = nc.dram_tensor("xn1T", [D, NTOK], BF16, kind="ExternalInput").ap()
    attn1 = nc.dram_tensor("attn1", [NTOK, D], BF16, kind="ExternalInput").ap()
    xa = nc.dram_tensor("xa", [NTOK, D], F32, kind="ExternalInput").ap()
    idb = nc.dram_tensor("idb", [128, 128], BF16, kind="ExternalInput").ap()
    out_o = nc.dram_tensor("out_o", [NTOK, D], F32, kind="ExternalOutput").ap()

    with ExitStack() as ctx:
        E = ctx.enter_context
        HT = E(nc.sbuf_tensor("HT", [128, NIT * 512], BF16))
        XNF = E(nc.sbuf_tensor("XNF", [128, NFD * 512], BF16))
        FFK = E(nc.sbuf_tensor("FFK", [128, 4 * D], BF16))
        WL = [E(nc.sbuf_tensor(f"WL{i}", [128, 512], BF16)) for i in range(2)]
        WDL = [E(nc.sbuf_tensor(f"WDL{i}", [128, 512], BF16)) for i in range(2)]
        SIG = E(nc.sbuf_tensor("SIG", [128, 1024], F32))
        SIDB = E(nc.sbuf_tensor("SIDB", [128, 128], BF16))
        FTB = [E(nc.sbuf_tensor(f"FTB{i}", [128, 512], BF16)) for i in range(2)]
        XAB = [E(nc.sbuf_tensor(f"XAB{i}", [128, 1024], F32)) for i in range(2)]
        ATB = [E(nc.sbuf_tensor(f"ATB{i}", [128, 1024], BF16)) for i in range(2)]
        OSB = [E(nc.sbuf_tensor(f"OSB{i}", [128, 1024], F32)) for i in range(2)]

        pG = E(nc.psum_tensor("pG", [128, 1024], F32))
        pU = E(nc.psum_tensor("pU", [128, 1024], F32))
        pT0 = E(nc.psum_tensor("pT0", [128, 1024], BF16))
        pT1 = E(nc.psum_tensor("pT1", [128, 1024], BF16))

        s_l = [E(nc.semaphore("s_l0")), E(nc.semaphore("s_l1"))]
        s_m = [E(nc.semaphore("s_m0")), E(nc.semaphore("s_m1"))]
        s_x = E(nc.semaphore("s_x"))
        s_c = E(nc.semaphore("s_c"))
        s_t = E(nc.semaphore("s_t"))
        s_v = E(nc.semaphore("s_v"))
        s_sc = E(nc.semaphore("s_sc"))
        s_st = [E(nc.semaphore("s_st0")), E(nc.semaphore("s_st1"))]
        s_bar = E(nc.semaphore("s_bar"))
        block = E(nc.Block())

        bar = [0]

        def barrier():
            bar[0] += 1
            tgt = 5 * bar[0]
            for dec in (block.sync, block.tensor, block.vector, block.scalar, block.gpsimd):
                @dec
                def _(e):
                    e.drain()
                    e.sem_inc(s_bar, 1)
                    e.wait_ge(s_bar, tgt)

        C = dict(t=0, v=0, sc=0, x=0, l=[0, 0], m=[0, 0], st=[0, 0])
        NIG = NIT // 2

        for th in range(2):
            T0, V0, SC0, X0 = C["t"], C["v"], C["sc"], C["x"]
            L0 = list(C["l"])

            @block.sync
            def _(sync, th=th):
                if th == 0:
                    sync.dma_start(out=SIDB[:], in_=idb).then_inc(s_c, 16)
                for f in range(NFD):
                    sync.dma_start(out=XNF[:, f * 512:(f + 1) * 512],
                                   in_=xn1T[f * 128:(f + 1) * 128, th * 512:(th + 1) * 512]
                                   ).then_inc(s_x, 16)
            C["x"] += 16 * NFD

            @block.sync
            def _(sync, th=th, T0=T0, L0=L0):
                for ig in range(NIG):
                    for kc in range(NFD):
                        n = ig * NFD + kc
                        par = n % 2
                        if n >= 2:
                            sync.wait_ge(s_t, T0 + (n - 2) + 1)
                        sync.dma_start(out=WL[par][:, 0:256],
                                       in_=Wg[ig * 256:(ig + 1) * 256, kc * 128:(kc + 1) * 128],
                                       transpose=True).then_inc(s_l[par], 16)
                        sync.dma_start(out=WL[par][:, 256:512],
                                       in_=Wu[ig * 256:(ig + 1) * 256, kc * 128:(kc + 1) * 128],
                                       transpose=True).then_inc(s_l[par], 16)

            @block.tensor
            def _(t, th=th, T0=T0, V0=V0, X0=X0, L0=L0):
                t.wait_ge(s_c, 16)
                t.wait_ge(s_x, X0 + 16 * NFD)
                for ig in range(NIG):
                    if ig >= 1:
                        t.wait_ge(s_v, V0 + 2 * ig)
                    for kc in range(NFD):
                        n = ig * NFD + kc
                        par = n % 2
                        t.wait_ge(s_l[par], L0[par] + 32 * (n // 2 + 1))
                        rhs = XNF[:, kc * 512:(kc + 1) * 512]
                        for i2 in range(2):
                            t.matmul(pG[:, i2 * 512:(i2 + 1) * 512],
                                     WL[par][:, i2 * 128:(i2 + 1) * 128], rhs,
                                     start=(kc == 0), stop=(kc == NFD - 1))
                            ins2 = t.matmul(pU[:, i2 * 512:(i2 + 1) * 512],
                                            WL[par][:, 256 + i2 * 128:256 + (i2 + 1) * 128], rhs,
                                            start=(kc == 0), stop=(kc == NFD - 1))
                            if i2 == 1:
                                ins2.then_inc(s_t, 1)

            @block.scalar
            def _(sc, th=th, T0=T0, V0=V0, SC0=SC0):
                for ig in range(NIG):
                    sc.wait_ge(s_t, T0 + ig * NFD + NFD)
                    for i2 in range(2):
                        ic2 = 2 * ig + i2
                        if ic2 >= 2:
                            sc.wait_ge(s_v, V0 + ic2 - 1)
                        sc.activation(SIG[:, i2 * 512:(i2 + 1) * 512],
                                      pG[:, i2 * 512:(i2 + 1) * 512], AF.Silu).then_inc(s_sc, 1)
                        sc.drain()

            @block.vector
            def _(v, th=th, SC0=SC0, V0=V0):
                for ig in range(NIG):
                    for i2 in range(2):
                        ic = ig * 2 + i2
                        v.wait_ge(s_sc, SC0 + ic + 1)
                        v.tensor_tensor(HT[:, ic * 512:(ic + 1) * 512],
                                        SIG[:, i2 * 512:(i2 + 1) * 512],
                                        pU[:, i2 * 512:(i2 + 1) * 512], op=AL.mult
                                        ).then_inc(s_v, 1)
                        v.drain()
            C["t"] += NIG * NFD
            C["sc"] += 2 * NIG
            C["v"] += 2 * NIG
            C["l"][0] += 32 * (NIG * NFD // 2)
            C["l"][1] += 32 * (NIG * NFD // 2)

            barrier()

            T0d, V0d = C["t"], C["v"]
            M0 = list(C["m"])

            def t_kc(hg, kc): return T0d + 102 * hg + kc + 1
            def t_mm(hg): return T0d + 102 * hg + NIT
            def t_tp(hg, i4, blk): return T0d + 102 * hg + NIT + 4 * i4 + blk + 1
            def v_ev(hg, i4): return V0d + 20 * hg + 5 * i4 + 1
            def v_tp(hg, i4, blk): return V0d + 20 * hg + 5 * i4 + 2 + blk

            @block.sync
            def _(sync, th=th, T0d=T0d, M0=M0):
                for hg in range(8):
                    for kc in range(NIT):
                        n = hg * NIT + kc
                        par = n % 2
                        if n >= 2:
                            hg2, kc2 = divmod(n - 2, NIT)
                            sync.wait_ge(s_t, t_kc(hg2, kc2))
                        sync.dma_start(out=WDL[par][:],
                                       in_=Wd[hg * 512:(hg + 1) * 512, kc * 128:(kc + 1) * 128],
                                       transpose=True).then_inc(s_m[par], 16)

            @block.tensor
            def _(t, th=th, T0d=T0d, V0d=V0d, M0=M0):
                for hg in range(8):
                    if hg >= 1:
                        t.wait_ge(s_v, v_ev(hg - 1, 3))
                    for kc in range(NIT):
                        n = hg * NIT + kc
                        par = n % 2
                        t.wait_ge(s_m[par], M0[par] + 16 * (n // 2 + 1))
                        rhs = HT[:, kc * 512:(kc + 1) * 512]
                        for i4 in range(4):
                            bank = (pG, pU)[i4 // 2]
                            ins = t.matmul(bank[:, (i4 % 2) * 512:(i4 % 2 + 1) * 512],
                                           WDL[par][:, i4 * 128:(i4 + 1) * 128], rhs,
                                           start=(kc == 0), stop=(kc == NIT - 1))
                            if i4 == 3:
                                ins.then_inc(s_t, 1)
                    for i4 in range(4):
                        t.wait_ge(s_v, v_ev(hg, i4))
                        for blk in range(4):
                            g = hg * 16 + 4 * i4 + blk
                            if g >= 2:
                                q = g - 2
                                hg2, r2 = divmod(q, 16)
                                i42, blk2 = divmod(r2, 4)
                                t.wait_ge(s_v, v_tp(hg2, i42, blk2))
                            pt = (pT0, pT1)[g % 2]
                            t.transpose(pt[:, 0:128], FTB[i4 % 2][:, blk * 128:(blk + 1) * 128],
                                        SIDB[:]).then_inc(s_t, 1)

            @block.vector
            def _(v, th=th, T0d=T0d, V0d=V0d):
                for hg in range(8):
                    for i4 in range(4):
                        v.wait_ge(s_t, t_mm(hg))
                        if i4 >= 2:
                            v.wait_ge(s_t, t_tp(hg, i4 - 2, 3))
                        elif hg >= 1:
                            v.wait_ge(s_t, t_tp(hg - 1, i4 + 2, 3))
                        bank = (pG, pU)[i4 // 2]
                        v.tensor_copy(FTB[i4 % 2][:], bank[:, (i4 % 2) * 512:(i4 % 2 + 1) * 512]
                                      ).then_inc(s_v, 1)
                        v.drain()
                        for blk in range(4):
                            v.wait_ge(s_t, t_tp(hg, i4, blk))
                            g = hg * 16 + 4 * i4 + blk
                            pt = (pT0, pT1)[g % 2]
                            hc = hg * 4 + i4
                            v.tensor_copy(FFK[:, blk * D + hc * 128: blk * D + (hc + 1) * 128],
                                          pt[:, 0:128]).then_inc(s_v, 1)
                            v.drain()
            C["t"] += 8 * 102
            C["v"] += 8 * 20
            C["m"][0] += 16 * (8 * NIT // 2)
            C["m"][1] += 16 * (8 * NIT - 8 * NIT // 2)

            barrier()

            V0f = C["v"]
            L0f = list(C["l"])
            S0f = list(C["st"])

            @block.sync
            def _(sync, th=th, V0f=V0f, S0f=S0f):
                for grp in range(16):
                    blk, pc = divmod(grp, 4)
                    par = grp % 2
                    if grp >= 2:
                        sync.wait_ge(s_v, V0f + grp - 1)
                    tr = slice(th * 512 + blk * 128, th * 512 + (blk + 1) * 128)
                    sync.dma_start(out=XAB[par][:], in_=xa[tr, pc * 1024:(pc + 1) * 1024]
                                   ).then_inc(s_l[par], 16)
                    sync.dma_start(out=ATB[par][:], in_=attn1[tr, pc * 1024:(pc + 1) * 1024]
                                   ).then_inc(s_l[par], 16)
                    if grp >= 1:
                        m = grp - 1
                        blk2, pc2 = divmod(m, 4)
                        tr2 = slice(th * 512 + blk2 * 128, th * 512 + (blk2 + 1) * 128)
                        sync.wait_ge(s_v, V0f + m + 1)
                        sync.dma_start(out=out_o[tr2, pc2 * 1024:(pc2 + 1) * 1024],
                                       in_=OSB[m % 2][:]).then_inc(s_st[m % 2], 16)
                m = 15
                blk2, pc2 = divmod(m, 4)
                tr2 = slice(th * 512 + blk2 * 128, th * 512 + (blk2 + 1) * 128)
                sync.wait_ge(s_v, V0f + m + 1)
                sync.dma_start(out=out_o[tr2, pc2 * 1024:(pc2 + 1) * 1024],
                               in_=OSB[m % 2][:]).then_inc(s_st[m % 2], 16)

            @block.vector
            def _(v, th=th, V0f=V0f, L0f=L0f, S0f=S0f):
                for grp in range(16):
                    blk, pc = divmod(grp, 4)
                    par = grp % 2
                    v.wait_ge(s_l[par], L0f[par] + 32 * (grp // 2 + 1))
                    v.wait_ge(s_st[par], S0f[par] + 16 * (grp // 2))
                    v.tensor_tensor(OSB[par][:], XAB[par][:], ATB[par][:], op=AL.add)
                    v.drain()
                    v.tensor_tensor(OSB[par][:], OSB[par][:],
                                    FFK[:, blk * D + pc * 1024: blk * D + (pc + 1) * 1024],
                                    op=AL.add).then_inc(s_v, 1)
                    v.drain()
            C["v"] += 16
            C["l"][0] += 32 * 8
            C["l"][1] += 32 * 8
            C["st"][0] += 16 * 8
            C["st"][1] += 16 * 8

            barrier()

        @block.gpsimd
        def _(g):
            g.wait_ge(s_st[0], C["st"][0])
            g.wait_ge(s_st[1], C["st"][1])

    return nc




def host_prep_D1(ins):
    rep8 = lambda a: np.ascontiguousarray(np.repeat(np.ascontiguousarray(a.T), 8, axis=0))
    return dict(
        qg=np.ascontiguousarray(ins["L1_ffn_gate_qweight"].T),
        qu=np.ascontiguousarray(ins["L1_ffn_up_qweight"].T),
        qd=np.ascontiguousarray(ins["L1_ffn_down_qweight"].T),
        sg=np.ascontiguousarray(ins["L1_ffn_gate_scales"].T),
        su=np.ascontiguousarray(ins["L1_ffn_up_scales"].T),
        sd=np.ascontiguousarray(ins["L1_ffn_down_scales"].T),
        zg=rep8(ins["L1_ffn_gate_qzeros"]),
        zu=rep8(ins["L1_ffn_up_qzeros"]),
        zd=rep8(ins["L1_ffn_down_qzeros"]),
        rmk=np.ascontiguousarray(
            (np.arange(128)[:, None] % 8 == np.arange(8)[None, :]).astype(np.uint8)),
    )



def kernel(**inputs):
    ins = inputs
    x0 = np.asarray(ins["L0_attn_hidden"], np.float32)

    ncA = build_A()
    resA = run_bass_kernel_spmd(ncA, [host_prep(ins, c) for c in range(8)],
                                core_ids=list(range(8)))
    a_res = [dict(resA.results[c]) for c in range(8)]

    ncB1 = build_B1()
    resB1 = run_bass_kernel_spmd(ncB1, [host_prep_B1(ins, c, a_res[c]) for c in range(8)],
                                 core_ids=list(range(8)))
    ssums = {c: np.asarray(resB1.results[c]["SSUM"]) for c in range(8)}

    ncB2 = build_B2()
    resB2 = run_bass_kernel_spmd(ncB2, [host_prep_B2(ins, c, a_res[c], ssums) for c in range(8)],
                                 core_ids=list(range(8)))
    x1s = [np.asarray(resB2.results[c]["x1_o"]) for c in range(8)]

    ncC = build_C()
    resC = run_bass_kernel_spmd(ncC, [host_prep_C(ins, c, x1s[c]) for c in range(8)],
                                core_ids=list(range(8)))
    c_res = [dict(resC.results[c]) for c in range(8)]

    ncD1 = build_D1()
    imD1 = host_prep_D1(ins)
    resD1 = run_bass_kernel_spmd(ncD1, [imD1] * 8, core_ids=list(range(8)))
    ws = {n: np.asarray(resD1.results[0][n]) for n in ["Wg_o", "Wu_o", "Wd_o"]}

    ncD2 = build_D2()

    def prep_d2(c):
        cr = c_res[c]
        return dict(
            Wg=ws["Wg_o"], Wu=ws["Wu_o"], Wd=ws["Wd_o"],
            xn1T=np.asarray(cr["xn1T_o"]),
            attn1=np.asarray(cr["attn1_o"]),
            xa=np.ascontiguousarray(x1s[c]),
            idb=np.eye(128, dtype=np.float32).astype(ml_dtypes.bfloat16),
        )
    resD2 = run_bass_kernel_spmd(ncD2, [prep_d2(c) for c in range(8)],
                                 core_ids=list(range(8)))
    outs = [np.asarray(resD2.results[c]["out_o"]).reshape(4, 256, 4096) for c in range(8)]
    return np.concatenate(outs, axis=1)
